# version 54
# baseline (speedup 1.0000x reference)
import contextlib
import numpy as np
import concourse.bass as bass
import concourse.mybir as mybir
from concourse.bass_utils import run_bass_kernel_spmd

F32 = mybir.dt.float32
BF16 = mybir.dt.bfloat16
I32 = mybir.dt.int32
AF = mybir.ActivationFunctionType
ALU = mybir.AluOpType

D, DC = 1024, 8
CI, CC = 1536, 12
H, HC = 3072, 24
L, TC = 2048, 16
NT, TTW = 4, 512
SCALE = 1.0 / 127.0
EPS = 1e-5
P = 128


class S:

    def __init__(self, nc, name):
        self.sem = nc.alloc_semaphore(name)
        self.n = 0

    def dma(self, inst):
        inst.then_inc(self.sem, 16)
        self.n += 16
        return self.n

    def inc(self, inst):
        inst.then_inc(self.sem, 1)
        self.n += 1
        return self.n


class RingS:

    def __init__(self, nc, name, depth):
        self.sems = [nc.alloc_semaphore(f"{name}{j}") for j in range(depth)]
        self.depth = depth
        self.n = 0

    def dma2(self, eng, fn):
        i = self.n // 16
        if i >= self.depth:
            eng.wait_ge(self.sems[i % self.depth], 16 * (i // self.depth))
        ins = fn()
        ins.then_inc(self.sems[i % self.depth], 16)
        self.n += 16
        return self.n

    def wait(self, eng, val):
        if val <= 0:
            return
        i = val // 16 - 1
        eng.wait_ge(self.sems[i % self.depth], 16 * (i // self.depth + 1))

    def wait_all(self, eng):
        total = self.n // 16
        for j in range(min(self.depth, total)):
            i = total - 1 - j
            eng.wait_ge(self.sems[i % self.depth], 16 * (i // self.depth + 1))


def build():
    nc = bass.Bass("TRN2", target_bir_lowering=False)

    x_in = nc.dram_tensor("x", [L, D], F32, kind="ExternalInput").ap()
    w_in = nc.dram_tensor("w_in", [H, D], F32, kind="ExternalInput").ap()
    w_out = nc.dram_tensor("w_out", [D, CI], F32, kind="ExternalInput").ap()
    conv_w = nc.dram_tensor("conv_w", [CI, 1, 3], F32, kind="ExternalInput").ap()
    conv_b = nc.dram_tensor("conv_b", [CI], F32, kind="ExternalInput").ap()
    dt_w = nc.dram_tensor("dt_w", [CI, CI], F32, kind="ExternalInput").ap()
    dt_b = nc.dram_tensor("dt_b", [CI], F32, kind="ExternalInput").ap()
    A_in = nc.dram_tensor("A", [CI], F32, kind="ExternalInput").ap()
    Dp_in = nc.dram_tensor("Dp", [CI], F32, kind="ExternalInput").ap()
    wg = nc.dram_tensor("wg", [H, D], I32, kind="ExternalInput").ap()
    wu = nc.dram_tensor("wu", [H, D], I32, kind="ExternalInput").ap()
    wd = nc.dram_tensor("wd", [D, H], I32, kind="ExternalInput").ap()
    out = nc.dram_tensor("out", [L, D], F32, kind="ExternalOutput").ap()

    xbf = nc.dram_tensor("xbf", [L, D], BF16).ap()
    wib = nc.dram_tensor("wib", [H, D], BF16).ap()
    dtwb = nc.dram_tensor("dtwb", [CI, CI], BF16).ap()
    wob = nc.dram_tensor("wob", [D, CI], BF16).ap()
    wgb = nc.dram_tensor("wgb", [H, D], BF16).ap()
    wub = nc.dram_tensor("wub", [H, D], BF16).ap()
    wdb = nc.dram_tensor("wdb", [D, H], BF16).ap()
    attnb = nc.dram_tensor("attnb", [D, L], BF16).ap()
    postb = nc.dram_tensor("postb", [D, L], BF16).ap()
    ffnb = nc.dram_tensor("ffnb", [D, L], BF16).ap()

    s_cast = RingS(nc, "s_cast", 4)
    s_dma = RingS(nc, "s_dma", 8)
    s_mm = S(nc, "s_mm")
    s_sc = S(nc, "s_sc")
    s_ve = S(nc, "s_ve")

    top = contextlib.ExitStack()
    with top:
        sb = lambda name, shape, dt: top.enter_context(nc.sbuf_tensor(name, shape, dt))
        cwk = sb("cwk", [P, 3, CC], F32)
        cbv = sb("cbv", [P, CC], F32)
        dbv = sb("dbv", [P, CC], F32)
        Av = sb("Av", [P, CC], F32)
        Dpv = sb("Dpv", [P, CC], F32)
        ones = sb("ones", [P, 1], F32)
        stats = sb("stats", [P, 2, 3], F32)
        cst = sb("cst", [P, 5], F32)
        ident = sb("ident", [P, P], BF16)
        zt = sb("zt", [P, P + 1], BF16)
        idd = nc.dram_tensor("idd", [P, P + 1], BF16).ap()
        cvals = [EPS, SCALE, -0.5, 1.0 / D, -1.0 / D]
        for i, vv in enumerate(cvals):
            nc.const_aps.register(F32, vv, cst[:, i:i + 1])
        psA = top.enter_context(nc.psum_tensor("psA", [P, L], F32))
        psB = top.enter_context(nc.psum_tensor("psB", [P, L], F32))
        ps = [psA, psB]

        m_cast = dict(wibL=16, wibR=32, wib2=48, dtwb=64, wob=80, wgb=96,
                      wub=112, wdb=128)

        pl_stack = contextlib.ExitStack()
        mid = contextlib.ExitStack()
        blkstack = contextlib.ExitStack()
        blk = blkstack.enter_context(nc.Block())

        pv = nc.alloc_semaphore("pv")
        m_pv = {}

        @blk.sync
        def _(s):
            s.wait_ge(pv, 1)
            m1 = s_dma.dma2(s, lambda: s.dma_start(out=idd, in_=zt[:]))
            s_dma.wait(s, m1)
            iv = idd.rearrange("p q -> (p q)").rearrange("(a b) -> a b", b=P)
            m_pv["ident"] = s_dma.dma2(s, lambda: s.dma_start(out=ident[:], in_=iv[0:P, :]))

        @blk.vector
        def _(v):
            v.memset(ones[:], 1.0)
            for i, vv in enumerate(cvals):
                v.memset(cst[:, i:i + 1], vv)
            ins = v.memset(zt[:, 1:P + 1], 0.0)
            ins.then_inc(s_ve.sem, 1)
            ins = v.memset(zt[:, 0:1], 1.0)
            ins.then_inc(pv, 1)
        s_ve.n += 1
        m_pv["ve"] = s_ve.n
        x_fm = pl_stack.enter_context(nc.sbuf_tensor("x_fm", [P, DC, L], BF16))
        with mid:
            yz = mid.enter_context(nc.sbuf_tensor("yz", [P, CC, L], BF16))
            ab = contextlib.ExitStack()
            with ab:
                x_conv = ab.enter_context(nc.sbuf_tensor("x_conv", [P, CC, L], BF16))
                casts = [(wib, w_in), (dtwb, dt_w), (wob, w_out),
                         (wgb, wg), (wub, wu), (wdb, wd)]
                for i, key in enumerate(["cwk", "cbv", "dbv", "Av", "Dpv"]):
                    m_pv[key] = s_dma.n + 32 * 16 + (i + 1) * 16
                mmA, scA = _phase_a(nc, blk, s_cast, s_dma, s_mm, s_sc, s_ve,
                                    m_cast, x_in, wib, x_conv, yz, cwk, cbv,
                                    ps, casts, x_fm, ident, m_pv)

                @blk.sync
                def _(s):
                    kw = dict(allow_slow_non_contiguous=True)
                    assert s_dma.n + 16 == m_pv["cwk"]
                    s_dma.dma2(s, lambda: s.dma_start(out=cwk[:], in_=conv_w.rearrange("(j p) o k -> p (o k) j", p=P), **kw))
                    s_dma.dma2(s, lambda: s.dma_start(out=cbv[:], in_=conv_b.rearrange("(j p) -> p j", p=P), **kw))
                    s_dma.dma2(s, lambda: s.dma_start(out=dbv[:], in_=dt_b.rearrange("(j p) -> p j", p=P), **kw))
                    s_dma.dma2(s, lambda: s.dma_start(out=Av[:], in_=A_in.rearrange("(j p) -> p j", p=P), **kw))
                    s_dma.dma2(s, lambda: s.dma_start(out=Dpv[:], in_=Dp_in.rearrange("(j p) -> p j", p=P), **kw))
                m_cast["_m_dtwT"] = s_dma.n + 12 * 16
                wo_stack = contextlib.ExitStack()
                woT = wo_stack.enter_context(
                    nc.sbuf_tensor("woT", [P, CC, D], BF16, side="right"))
                m_woT, seamB = _phase_b(nc, blk, s_cast, s_dma, s_mm, s_sc,
                                        s_ve, m_cast, dtwb, x_conv, yz, dbv,
                                        Av, Dpv, ones, ps,
                                        [casts[4], casts[5]], woT, wob,
                                        mmA, scA, m_pv)
            plnorm, seamC = _phase_c(nc, blk, s_cast, s_dma, s_mm, s_sc, s_ve,
                                     m_cast, wob, x_fm, yz, woT, m_woT,
                                     ones, cst, ps, seamB)
            wo_stack.close()

        with pl_stack:
            dstk = contextlib.ExitStack()
            with dstk:
                h_fm = dstk.enter_context(nc.sbuf_tensor("h_fm", [P, HC, L], BF16))
                wdh = [dstk.enter_context(nc.sbuf_tensor(f"wdh{i}", [P, HC, L // 4], BF16)) for i in range(2)]
                seamD = _phase_d1(nc, blk, s_cast, s_dma, s_mm, s_sc, s_ve, m_cast,
                                  plnorm, wgb, wub, x_fm, h_fm, psA, psB,
                                  wdb, wdh, seamB, seamC)
                _phase_d2e(nc, blk, s_cast, s_dma, s_mm, s_sc, s_ve, m_cast,
                           wdb, ffnb, h_fm, x_fm, out, ps, wdh, seamD, ident)
            blkstack.close()

    return nc


def _phase_a(nc, blk, s_cast, s_dma, s_mm, s_sc, s_ve, m_cast, x_in, wib,
             x_conv, yz, cwk, cbv, ps, casts, x_fm, ident, m_pv):
    stk = contextlib.ExitStack()
    with stk:
        wiT = stk.enter_context(nc.sbuf_tensor("wiT", [P, DC, CI], BF16))
        wiT2 = stk.enter_context(nc.sbuf_tensor("wiT2", [P, DC, CI], BF16))
        stg1a = stk.enter_context(nc.sbuf_tensor("stg1a", [P, L], F32))
        stg1b = stk.enter_context(nc.sbuf_tensor("stg1b", [P, L], F32))
        stg2 = stk.enter_context(nc.sbuf_tensor("stg2", [P, L], F32))
        stg1 = [stg1a, stg1b]
        xtm = nc.alloc_sbuf_tensor_at("xtm", (P, TC, D), BF16,
                                      offset=nc.lookup_mloc(yz).addr)
        xf32 = nc.alloc_sbuf_tensor_at("xf32", (P, 4, D), F32,
                                       offset=nc.lookup_mloc(stg1a).addr)

        xld = [s_dma.n + 16 * (c + 1) for c in range(TC)]
        trmm = [s_mm.n + c + 1 for c in range(TC)]
        cast_ms, evac_ms = [0] * TC, [0] * TC
        n = s_sc.n
        sc_plan = []
        for c in range(TC):
            sc_plan.append(("C", c))
            if c >= 3:
                sc_plan.append(("E", c - 3))
        sc_plan += [("E", c) for c in range(TC - 3, TC)]
        for op, c in sc_plan:
            n += 1
            (cast_ms if op == "C" else evac_ms)[c] = n
        mmA = [trmm[-1] + m + 1 for m in range(24)]
        veA = [s_ve.n + m + 1 for m in range(12)]
        scA = [n + m + 1 for m in range(24)]
        trA = [xld[-1] + 16 * (k + 1) for k in range(DC)]
        m_wiT1 = trA[-1]
        m_wiT2 = m_wiT1 + 8 * 16
        m_cast["_m_dtwT_plan"] = m_wiT2 + 12 * 16

        if True:
            @blk.gpsimd
            def _(g):
                (wib_d, wsrc) = casts[0]
                s_cast.dma2(g, lambda: g.dma_start(out=wib_d[0:CI, 0:D // 2], in_=wsrc[0:CI, 0:D // 2]))
                s_cast.dma2(g, lambda: g.dma_start(out=wib_d[0:CI, D // 2:D], in_=wsrc[0:CI, D // 2:D]))
                s_dma.wait(g, m_wiT1)
                s_cast.dma2(g, lambda: g.dma_start(out=wib_d[CI:H, :], in_=wsrc[CI:H, :]))
                s_dma.wait(g, m_wiT2)
                s_cast.dma2(g, lambda: g.dma_start(out=casts[1][0], in_=casts[1][1]))
                s_cast.dma2(g, lambda: g.dma_start(out=casts[2][0], in_=casts[2][1]))
                s_cast.dma2(g, lambda: g.dma_start(out=casts[3][0], in_=casts[3][1]))

            @blk.sync
            def _(s):
                for c in range(TC):
                    if c >= 4:
                        s.wait_ge(s_sc.sem, cast_ms[c - 4])
                    s_dma.dma2(s, lambda: s.dma_start(
                        out=xf32[:, c % 4, :], in_=x_in[c * P:(c + 1) * P, :]))
                for k in range(DC):
                    s_cast.wait(s, m_cast["wibL" if k < DC // 2 else "wibR"])
                    s_dma.dma2(s, lambda: s.dma_start_transpose(wiT[:, k, :], wib[0:CI, k * P:(k + 1) * P]))
                s_cast.wait(s, m_cast["wib2"])
                for k in range(DC):
                    s_dma.dma2(s, lambda: s.dma_start_transpose(wiT2[:, k, :], wib[CI:H, k * P:(k + 1) * P]))

            @blk.tensor
            def _(t):
                s_dma.wait(t, m_pv["ident"])
                for c in range(TC):
                    t.wait_ge(s_sc.sem, cast_ms[c])
                    if c >= 2:
                        t.wait_ge(s_sc.sem, evac_ms[c - 2])
                    base = (c % 2) * (L // 2)
                    ins = None
                    for k in range(DC):
                        ins = t.matmul(
                            ps[0][:, base + k * P:base + (k + 1) * P],
                            xtm[:, c, k * P:(k + 1) * P], ident[:],
                            start=True, stop=True)
                    ins.then_inc(s_mm.sem, 1)
                for m in range(24):
                    mm = m % 12
                    pp = m % 2
                    wt = wiT if m < 12 else wiT2
                    if m == 0:
                        t.wait_ge(s_sc.sem, evac_ms[-1])
                    if 2 <= m < 14:
                        t.wait_ge(s_ve.sem, veA[m - 2])
                    elif m >= 14:
                        t.wait_ge(s_sc.sem, scA[m - 2])
                    ins = None
                    for k in range(DC):
                        if m == 0:
                            s_dma.wait(t, trA[k])
                        elif m == 12:
                            s_dma.wait(t, m_wiT1 + (k + 1) * 16)
                        for tt in range(NT):
                            ins = t.matmul(
                                ps[pp][:, tt * TTW:(tt + 1) * TTW],
                                wt[:, k, mm * P:(mm + 1) * P],
                                x_fm[:, k, tt * TTW:(tt + 1) * TTW],
                                start=(k == 0), stop=(k == DC - 1),
                            )
                    ins.then_inc(s_mm.sem, 1)

            @blk.vector
            def _(v):
                s_dma.wait(v, m_pv["cwk"])
                for m in range(12):
                    pp = m % 2
                    if m >= 2:
                        v.wait_ge(s_sc.sem, scA[m - 2])
                    v.wait_ge(s_mm.sem, mmA[m])
                    v.tensor_scalar(stg1[pp][:, 1:L], ps[pp][:, 0:L - 1],
                                    cwk[:, 0, m:m + 1], None, ALU.mult)
                    v.tensor_scalar(stg2[:, 0:1], ps[pp][:, 0:1],
                                    cwk[:, 1, m:m + 1], None, ALU.mult)
                    v.scalar_tensor_tensor(stg2[:, 1:L], ps[pp][:, 1:L],
                                           cwk[:, 1, m:m + 1],
                                           stg1[pp][:, 1:L], ALU.mult, ALU.add)
                    v.scalar_tensor_tensor(stg1[pp][:, 0:L - 1], ps[pp][:, 1:L],
                                           cwk[:, 2, m:m + 1], stg2[:, 0:L - 1],
                                           ALU.mult, ALU.add)
                    ins = v.tensor_copy(stg1[pp][:, L - 1:L], stg2[:, L - 1:L])
                    ins.then_inc(s_ve.sem, 1)

            @blk.scalar
            def _(sc):
                for op, c in sc_plan:
                    if op == "C":
                        s_dma.wait(sc, xld[c])
                        ins = sc.activation(xtm[:, c, :], xf32[:, c % 4, :],
                                            AF.Copy)
                    else:
                        sc.wait_ge(s_mm.sem, trmm[c])
                        base = (c % 2) * (L // 2)
                        for k in range(DC):
                            ins = sc.activation(
                                x_fm[:, k, c * P:(c + 1) * P],
                                ps[0][:, base + k * P:base + (k + 1) * P],
                                AF.Copy)
                    ins.then_inc(s_sc.sem, 1)
                s_dma.wait(sc, m_pv["cbv"])
                for m in range(12):
                    pp = m % 2
                    sc.wait_ge(s_ve.sem, veA[m])
                    ins = sc.activation(x_conv[:, m, :], stg1[pp][:], AF.Silu,
                                        bias=cbv[:, m:m + 1])
                    ins.then_inc(s_sc.sem, 1)
                for m in range(12, 24):
                    pp = m % 2
                    sc.wait_ge(s_mm.sem, mmA[m])
                    ins = sc.activation(yz[:, m - 12, :], ps[pp][:, :], AF.Silu)
                    ins.then_inc(s_sc.sem, 1)

        s_mm.n = mmA[-1]
        s_ve.n = veA[-1]
        s_sc.n = scA[-1]
        s_dma.n = m_wiT2
        return mmA, scA


def _phase_b(nc, blk, s_cast, s_dma, s_mm, s_sc, s_ve, m_cast, dtwb,
             x_conv, yz, dbv, Av, Dpv, ones, ps, late_casts, woT, wob,
             mmA, scA, m_pv):
    psA, psB = ps
    HF = L // 2
    stk = contextlib.ExitStack()
    with stk:
        dtwT = stk.enter_context(nc.sbuf_tensor("dtwT", [P, CC, CI], BF16))
        bsA = [stk.enter_context(nc.sbuf_tensor(f"bsA{i}", [P, HF], BF16)) for i in range(2)]
        bsB = [stk.enter_context(nc.sbuf_tensor(f"bsB{i}", [P, HF], BF16)) for i in range(2)]
        bsC = [stk.enter_context(nc.sbuf_tensor(f"bsC{i}", [P, HF], BF16)) for i in range(2)]
        carryv = stk.enter_context(nc.sbuf_tensor("carryv", [P, CC], F32))

        pr = [psA[:, 0:HF], psA[:, HF:L], psB[:, 0:HF], psB[:, HF:L]]

        mmB = [s_mm.n + g + 1 for g in range(24)]
        pfree = [s_sc.n + 2 * g + 1 for g in range(24)]
        ready = [s_sc.n + 2 * g + 2 for g in range(24)]
        veB = [s_ve.n + g + 1 for g in range(24)]
        m_dtwT = [s_dma.n + (k + 1) * 16 for k in range(CC)]
        m_woT = [m_dtwT[-1] + (k + 1) * 16 for k in range(CC)]
        ones_hb = ones[:, 0:1].broadcast_to((P, HF))

        if True:
            @blk.gpsimd
            def _(g):
                s_dma.wait(g, m_woT[-1])
                for dst, srcp in late_casts:
                    s_cast.dma2(g, lambda: g.dma_start(out=dst, in_=srcp))

            @blk.sync
            def _(s):
                s_cast.wait(s, m_cast["dtwb"])
                s.wait_ge(s_mm.sem, mmA[23])
                for k in range(CC):
                    s_dma.dma2(s, lambda: s.dma_start_transpose(
                        dtwT[:, k, :], dtwb[0:CI, k * P:(k + 1) * P]))
                s_cast.wait(s, m_cast["wob"])
                for k in range(CC):
                    s_dma.dma2(s, lambda: s.dma_start_transpose(
                        woT[:, k, :], wob[0:D, k * P:(k + 1) * P]))

            bpar = lambda g: (g % 4) if g < 20 else 2 + (g % 2)

            @blk.tensor
            def _(t):
                for g in range(24):
                    h, m = divmod(g, 12)
                    par = bpar(g)
                    if g < 2:
                        t.wait_ge(s_sc.sem, scA[22])
                    elif g < 4:
                        t.wait_ge(s_sc.sem, scA[23])
                    elif g < 20:
                        t.wait_ge(s_sc.sem, pfree[g - 4])
                    else:
                        t.wait_ge(s_sc.sem, pfree[g - 2])
                    ins = None
                    for k in range(CC):
                        if g == 0:
                            t.wait_ge(s_sc.sem, scA[k])
                        if h == 0 and m == 0:
                            s_dma.wait(t, m_dtwT[k])
                        for tt in range(2):
                            ins = t.matmul(
                                pr[par][:, tt * TTW:(tt + 1) * TTW],
                                dtwT[:, k, m * P:(m + 1) * P],
                                x_conv[:, k, h * HF + tt * TTW:h * HF + (tt + 1) * TTW],
                                start=(k == 0), stop=(k == CC - 1),
                            )
                    ins.then_inc(s_mm.sem, 1)

            @blk.scalar
            def _(sc):
                s_dma.wait(sc, m_pv["dbv"])
                s_dma.wait(sc, m_pv["Av"])
                sc.wait_ge(s_ve.sem, m_pv["ve"])
                for g in range(24):
                    h, m = divmod(g, 12)
                    par = bpar(g)
                    ss = g % 2
                    sc.wait_ge(s_mm.sem, mmB[g])
                    if g >= 2:
                        sc.wait_ge(s_ve.sem, veB[g - 2])
                    ins = sc.activation(bsA[ss][:], pr[par], AF.Exp,
                                        bias=dbv[:, m:m + 1])
                    ins.then_inc(s_sc.sem, 1)
                    sc.activation(bsB[ss][:], bsA[ss][:], AF.Ln, bias=1.0)
                    ins = sc.activation(bsC[ss][:], bsB[ss][:], AF.Exp,
                                        scale=Av[:, m:m + 1])
                    ins.then_inc(s_sc.sem, 1)

            @blk.vector
            def _(v):
                s_dma.wait(v, m_pv["Dpv"])
                for g in range(24):
                    h, m = divmod(g, 12)
                    ss = g % 2
                    xs = x_conv[:, m, h * HF:(h + 1) * HF]
                    v.wait_ge(s_sc.sem, ready[g])
                    v.tensor_mul(bsA[ss][:], bsB[ss][:], xs)
                    v.tensor_mul(bsA[ss][:], bsA[ss][:], bsC[ss][:])
                    v.tensor_tensor_scan(bsB[ss][:], ones_hb, bsA[ss][:], 0.0,
                                         ALU.mult, ALU.add)
                    if h == 0:
                        v.tensor_scalar(carryv[:, m:m + 1],
                                        bsB[ss][:, HF - 1:HF], ones[:, 0:1],
                                        None, ALU.mult)
                    else:
                        v.tensor_scalar(bsB[ss][:], bsB[ss][:], carryv[:, m:m + 1],
                                        None, ALU.add)
                    v.tensor_mul(bsB[ss][:], bsB[ss][:], bsC[ss][:])
                    v.tensor_scalar(bsA[ss][:], xs, Dpv[:, m:m + 1], None, ALU.mult)
                    v.tensor_add(bsB[ss][:], bsB[ss][:], bsA[ss][:])
                    ins = v.tensor_mul(yz[:, m, h * HF:(h + 1) * HF], bsB[ss][:],
                                       yz[:, m, h * HF:(h + 1) * HF])
                    ins.then_inc(s_ve.sem, 1)

        s_mm.n = mmB[-1]
        s_ve.n = veB[-1]
        s_sc.n = ready[-1]
        s_dma.n = m_woT[-1]
        return m_woT, dict(mmB=mmB, veB=veB, pfree=pfree, ready=ready)


def _phase_c(nc, blk, s_cast, s_dma, s_mm, s_sc, s_ve, m_cast,
             wob, x_fm, yz, woT, m_woT, ones, cst, ps, seamB):
    psA, psB = ps
    HF = L // 2
    stk = contextlib.ExitStack()
    with stk:
        sq1 = stk.enter_context(nc.sbuf_tensor("sq1", [P, 2, HF], BF16))
        onesb = stk.enter_context(nc.sbuf_tensor("onesb", [P, 1], BF16))
        onesr = stk.enter_context(nc.sbuf_tensor("onesr", [1, P], BF16))
        vart = stk.enter_context(nc.sbuf_tensor("vart", [1, L], F32))
        meant = stk.enter_context(nc.sbuf_tensor("meant", [1, L], BF16))
        rstdt = stk.enter_context(nc.sbuf_tensor("rstdt", [1, L], BF16))
        mbc = stk.enter_context(nc.sbuf_tensor("mbc", [P, L], BF16))
        rbc = stk.enter_context(nc.sbuf_tensor("rbc", [P, L], BF16))

        pmm = [psA[:, 0:HF], psA[:, HF:L]]
        pS = psB[0:1, 0:HF]
        pQ = psB[0:1, HF:L]

        t_ops = []
        pend = []
        NG = 16
        for g in range(NG + 6):
            if g < NG:
                t_ops.append(("G",) + divmod(g, 8))
            for it in list(pend):
                if it[3] <= g:
                    t_ops.append(it[:3])
                    pend.remove(it)
            if g < NG:
                h, m = divmod(g, 8)
                pend.append(("S", h, m, g + 2))
                if g == NG - 1:
                    pend.append(("BC", 0, 0, g + 4))
        mmC, stC = {}, {}
        bcastC = [0]
        n = s_mm.n
        for op in t_ops:
            n += 1
            if op[0] == "G":
                mmC[(op[1], op[2])] = n
            elif op[0] == "S":
                stC[(op[1], op[2])] = n
            else:
                bcastC[0] = n
        bcastC = bcastC[0]

        v0 = s_ve.n + 1
        residC, mcC = {}, {}
        n = v0
        for g in range(NG):
            h, m = divmod(g, 8)
            n += 1
            residC[(h, m)] = n
            if m == 7:
                n += 1
                mcC[h] = n
        bcevC = n + 1
        normC = [bcevC + m + 1 for m in range(8)]
        sqC, leC = {}, {}
        n = s_sc.n
        for g in range(NG):
            h, m = divmod(g, 8)
            n += 1
            sqC[(h, m)] = n
            if m == 7:
                n += 1
                leC[h] = n
        if True:
            @blk.tensor
            def _(t):
                for op in t_ops:
                    kind, h, m = op
                    if kind == "G":
                        g = h * 8 + m
                        par = g % 2
                        if g < 2:
                            t.wait_ge(s_sc.sem, seamB["pfree"][16 + g])
                        else:
                            prev = divmod(g - 2, 8)
                            t.wait_ge(s_ve.sem, residC[prev])
                        ins = None
                        for k in range(CC):
                            if g == 0:
                                s_dma.wait(t, m_woT[k])
                            if m == 0:
                                t.wait_ge(s_ve.sem, seamB["veB"][h * 12 + k])
                            for tt in range(2):
                                ins = t.matmul(
                                    pmm[par][:, tt * TTW:(tt + 1) * TTW],
                                    woT[:, k, m * P:(m + 1) * P],
                                    yz[:, k, h * HF + tt * TTW:h * HF + (tt + 1) * TTW],
                                    start=(k == 0), stop=(k == CC - 1),
                                )
                        ins.then_inc(s_mm.sem, 1)
                    elif kind == "S":
                        g = h * 8 + m
                        t.wait_ge(s_ve.sem, residC[(h, m)])
                        t.wait_ge(s_sc.sem, sqC[(h, m)])
                        if h == 0 and m == 0:
                            t.wait_ge(s_sc.sem, seamB["pfree"][23])
                        if h == 1 and m == 0:
                            t.wait_ge(s_ve.sem, mcC[0])
                        ins = None
                        for tt in range(2):
                            t.matmul(pS[:, tt * TTW:(tt + 1) * TTW], onesb[:, 0:1],
                                     x_fm[:, m, h * HF + tt * TTW:h * HF + (tt + 1) * TTW],
                                     start=(m == 0), stop=(m == 7))
                            ins = t.matmul(pQ[:, tt * TTW:(tt + 1) * TTW], onesb[:, 0:1],
                                           sq1[:, g % 2, tt * TTW:(tt + 1) * TTW],
                                           start=(m == 0), stop=(m == 7))
                        ins.then_inc(s_mm.sem, 1)
                    else:
                        t.wait_ge(s_sc.sem, leC[1])
                        for tt in range(NT):
                            t.matmul(psA[:, tt * TTW:(tt + 1) * TTW], onesr[:, :],
                                     meant[:, tt * TTW:(tt + 1) * TTW], start=True, stop=True)
                        ins = None
                        for tt in range(NT):
                            ins = t.matmul(psB[:, tt * TTW:(tt + 1) * TTW], onesr[:, :],
                                           rstdt[:, tt * TTW:(tt + 1) * TTW], start=True, stop=True)
                        ins.then_inc(s_mm.sem, 1)

            @blk.vector
            def _(v):
                v.memset(onesb[:], 1.0)
                ins = v.memset(onesr[:], 1.0)
                ins.then_inc(s_ve.sem, 1)
                for g in range(NG):
                    h, m = divmod(g, 8)
                    par = g % 2
                    v.wait_ge(s_mm.sem, mmC[(h, m)])
                    ins = v.tensor_add(x_fm[:, m, h * HF:(h + 1) * HF], pmm[par],
                                       x_fm[:, m, h * HF:(h + 1) * HF])
                    ins.then_inc(s_ve.sem, 1)
                    if m == 7:
                        v.wait_ge(s_mm.sem, stC[(h, 7)])
                        v.tensor_scalar(meant[:, h * HF:(h + 1) * HF], pS, 1.0 / D,
                                        None, ALU.mult)
                        v.tensor_mul(vart[:, h * HF:(h + 1) * HF],
                                     meant[:, h * HF:(h + 1) * HF],
                                     meant[:, h * HF:(h + 1) * HF])
                        ins = v.scalar_tensor_tensor(vart[:, h * HF:(h + 1) * HF], pQ,
                                                     1.0 / D, vart[:, h * HF:(h + 1) * HF],
                                                     ALU.mult, ALU.subtract)
                        ins.then_inc(s_ve.sem, 1)
                v.wait_ge(s_mm.sem, bcastC)
                v.tensor_copy(mbc[:], psA[:, :])
                ins = v.tensor_copy(rbc[:], psB[:, :])
                ins.then_inc(s_ve.sem, 1)
                for m in range(8):
                    v.tensor_sub(x_fm[:, m, :], x_fm[:, m, :], mbc[:])
                    ins = v.tensor_mul(x_fm[:, m, :], x_fm[:, m, :], rbc[:])
                    ins.then_inc(s_ve.sem, 1)

            @blk.scalar
            def _(sc):
                for g in range(NG):
                    h, m = divmod(g, 8)
                    sc.wait_ge(s_ve.sem, residC[(h, m)])
                    if g >= 2:
                        sc.wait_ge(s_mm.sem, stC[divmod(g - 2, 8)])
                    ins = sc.activation(sq1[:, g % 2, :],
                                        x_fm[:, m, h * HF:(h + 1) * HF], AF.Square)
                    ins.then_inc(s_sc.sem, 1)
                    if m == 7:
                        sc.wait_ge(s_ve.sem, mcC[h])
                        sc.activation(vart[:, h * HF:(h + 1) * HF],
                                      vart[:, h * HF:(h + 1) * HF], AF.Ln, bias=EPS)
                        ins = sc.activation(rstdt[:, h * HF:(h + 1) * HF],
                                            vart[:, h * HF:(h + 1) * HF],
                                            AF.Exp, scale=-0.5)
                        ins.then_inc(s_sc.sem, 1)

        s_mm.n = bcastC
        s_ve.n = normC[-1]
        s_sc.n = leC[1]
        return normC, dict(bcastC=bcastC, bcevC=bcevC, leC=leC,
                           mmClast=mmC[(1, 7)])


def _phase_d1(nc, blk, s_cast, s_dma, s_mm, s_sc, s_ve, m_cast, plnorm,
              wgb, wub, pl_fm, h_fm, psA, psB, wdb, wdh, seamB, seamC):
    QN, QM = 8, 3
    QH = 384
    stk = contextlib.ExitStack()
    with stk:
        wgq = [stk.enter_context(nc.sbuf_tensor(f"wgq{i}", [P, DC, QH], BF16)) for i in range(2)]
        wuq = [stk.enter_context(nc.sbuf_tensor(f"wuq{i}", [P, DC, QH], BF16)) for i in range(2)]
        gst = stk.enter_context(nc.sbuf_tensor("gst", [P, L], BF16))

        gdone = [s_mm.n + 2 * m + 1 for m in range(24)]
        udone = [s_mm.n + 2 * m + 2 for m in range(24)]
        hdone = [s_ve.n + m + 1 for m in range(24)]
        siluD = [s_sc.n + m + 1 for m in range(24)]
        base = s_dma.n
        wg_str = [[base + q * 256 + (k + 1) * 16 for k in range(DC)] for q in range(QN)]
        wu_str = [[base + q * 256 + 128 + (k + 1) * 16 for k in range(DC)] for q in range(QN)]
        wd0 = base + QN * 256
        wd_str = [[wd0 + hh * HC * 16 + (k + 1) * 16 for k in range(HC)]
                  for hh in range(2)]

        if True:
            @blk.sync
            def _(s):
                s_cast.wait(s, m_cast["wub"])
                s.wait_ge(s_ve.sem, seamB["veB"][23])
                s.wait_ge(s_sc.sem, seamB["ready"][23])
                s.wait_ge(s_mm.sem, seamC["mmClast"])
                for q in range(QN):
                    if q >= 2:
                        s.wait_ge(s_mm.sem, udone[QM * q - QM - 1])
                    for k in range(DC):
                        s_dma.dma2(s, lambda: s.dma_start_transpose(
                            wgq[q % 2][:, k, :], wgb[q * QH:(q + 1) * QH, k * P:(k + 1) * P]))
                    for k in range(DC):
                        s_dma.dma2(s, lambda: s.dma_start_transpose(
                            wuq[q % 2][:, k, :], wub[q * QH:(q + 1) * QH, k * P:(k + 1) * P]))
                s_cast.wait(s, m_cast["wdb"])
                for hh in range(2):
                    for k in range(HC):
                        s_dma.dma2(s, lambda: s.dma_start_transpose(
                            wdh[hh][:, k, :],
                            wdb[hh * (D // 2):(hh + 1) * (D // 2), k * P:(k + 1) * P]))

            @blk.tensor
            def _(t):
                for m in range(24):
                    q, mq = m // QM, m % QM
                    par = q % 2
                    if m >= 1:
                        t.wait_ge(s_sc.sem, siluD[m - 1])
                    ins = None
                    for k in range(DC):
                        if mq == 0:
                            s_dma.wait(t, wg_str[q][k])
                            if m == 0:
                                t.wait_ge(s_ve.sem, plnorm[k])
                        for tt in range(NT):
                            ins = t.matmul(
                                psA[:, tt * TTW:(tt + 1) * TTW],
                                wgq[par][:, k, mq * P:(mq + 1) * P],
                                pl_fm[:, k, tt * TTW:(tt + 1) * TTW],
                                start=(k == 0), stop=(k == DC - 1),
                            )
                    ins.then_inc(s_mm.sem, 1)
                    if m >= 1:
                        t.wait_ge(s_ve.sem, hdone[m - 1])
                    for k in range(DC):
                        if mq == 0:
                            s_dma.wait(t, wu_str[q][k])
                        for tt in range(NT):
                            ins = t.matmul(
                                psB[:, tt * TTW:(tt + 1) * TTW],
                                wuq[par][:, k, mq * P:(mq + 1) * P],
                                pl_fm[:, k, tt * TTW:(tt + 1) * TTW],
                                start=(k == 0), stop=(k == DC - 1),
                            )
                    ins.then_inc(s_mm.sem, 1)

            @blk.scalar
            def _(sc):
                for m in range(24):
                    sc.wait_ge(s_mm.sem, gdone[m])
                    if m >= 1:
                        sc.wait_ge(s_ve.sem, hdone[m - 1])
                    ins = sc.activation(gst[:], psA[:, :], AF.Silu,
                                        scale=SCALE)
                    ins.then_inc(s_sc.sem, 1)

            @blk.vector
            def _(v):
                for m in range(24):
                    v.wait_ge(s_sc.sem, siluD[m])
                    v.wait_ge(s_mm.sem, udone[m])
                    ins = v.scalar_tensor_tensor(h_fm[:, m, :], psB[:, :],
                                                 SCALE, gst[:],
                                                 ALU.mult, ALU.mult)
                    ins.then_inc(s_ve.sem, 1)

        s_mm.n = udone[-1]
        s_ve.n = hdone[-1]
        s_sc.n = siluD[-1]
        s_dma.n = wd_str[-1][-1]
        return dict(hdone=hdone, siluD=siluD, udone=udone, wd_str=wd_str)


def _phase_d2e(nc, blk, s_cast, s_dma, s_mm, s_sc, s_ve, m_cast,
               wdb, outb, h_fm, pl_fm, out, ps, wdh, seamD, ident):
    psA, psB = ps
    NTB = 4
    TB = L // NTB
    NSL = 12
    stk = contextlib.ExitStack()
    with stk:
        r2f = stk.enter_context(nc.sbuf_tensor("r2f", [P, NSL, TB], BF16))
        sq2 = stk.enter_context(nc.sbuf_tensor("sq2", [P, 2, TB], BF16))
        ofb = stk.enter_context(nc.sbuf_tensor("ofb", [P, 4, TB], BF16))
        otm = stk.enter_context(nc.sbuf_tensor("otm", [P, 2, D], BF16))
        onesb2 = stk.enter_context(nc.sbuf_tensor("onesb2", [P, 1], BF16))
        onesr2 = stk.enter_context(nc.sbuf_tensor("onesr2", [1, P], BF16))
        meant2 = stk.enter_context(nc.sbuf_tensor("meant2", [1, TB], BF16))
        rstd2 = stk.enter_context(nc.sbuf_tensor("rstd2", [1, TB], BF16))
        vart2 = stk.enter_context(nc.sbuf_tensor("vart2", [1, TB], F32))
        otf = nc.alloc_sbuf_tensor_at("otf", (P, 1, D), F32,
                                      offset=nc.lookup_mloc(otm).addr)
        ot3 = nc.alloc_sbuf_tensor_at("ot3", (P, 1, D), BF16,
                                      offset=nc.lookup_mloc(sq2).addr)

        pmm = [psA[:, 0:TB], psA[:, TB:2 * TB]]
        strip = [psA[:, 0:2 * TB], psB[:, 0:2 * TB], psB[:, 2 * TB:4 * TB]]
        pMB = psA[:, 2 * TB:3 * TB]
        pRB = psA[:, 3 * TB:4 * TB]
        pS = psB[0:1, 0:TB]
        pQ = psB[0:1, TB:2 * TB]
        slot = lambda tb, m: (tb * 8 + m) % NSL

        t_ops = []
        pend = []
        for gm in range(NTB * 8 + 6):
            if gm < NTB * 8:
                t_ops.append(("G",) + divmod(gm, 8))
            for it in list(pend):
                if it[3] <= gm:
                    t_ops.append(it[:3])
                    pend.remove(it)
            tb, m = divmod(gm, 8)
            if gm < NTB * 8:
                pend.append(("S", tb, m, gm + 2))
                if m == 7:
                    pend.append(("BC", tb, 0, gm + 4))
        mmF, stF, bcF = {}, {}, {}
        n = s_mm.n
        for op in t_ops:
            n += 1
            if op[0] == "G":
                mmF[(op[1], op[2])] = n
            elif op[0] == "S":
                stF[(op[1], op[2])] = n
            else:
                bcF[op[1]] = n

        v_ops = []
        for tb in range(NTB):
            if tb == 0:
                v_ops += [("EV", 0, m) for m in range(8)]
                v_ops.append(("MC", 0, 0))
            else:
                v_ops += [("EV", tb, 0), ("EV", tb, 1), ("EV", tb, 2),
                          ("NM", tb - 1, 0), ("NM", tb - 1, 1), ("EV", tb, 3),
                          ("NM", tb - 1, 2), ("NM", tb - 1, 3), ("EV", tb, 4),
                          ("NM", tb - 1, 4), ("NM", tb - 1, 5), ("EV", tb, 5),
                          ("EV", tb, 6),
                          ("NM", tb - 1, 6), ("NM", tb - 1, 7), ("EV", tb, 7)]
                v_ops.append(("MC", tb, 0))
        v_ops += [("NM", NTB - 1, m) for m in range(8)]
        evF, vvF, nmF = {}, {}, {}
        n = s_ve.n + 1
        for op in v_ops:
            n += 1
            d = {"EV": evF, "NM": nmF}.get(op[0])
            if d is not None:
                d[(op[1], op[2])] = n
            else:
                vvF[op[1]] = n

        c_ops = []
        for tb in range(NTB):
            if tb == 0:
                c_ops += [("SQ", 0, m) for m in range(8)]
            else:
                c_ops += [("SQ", tb, 0), ("SQ", tb, 1)]
                c_ops.append(("LE", tb - 1, 0))
                c_ops += [("SQ", tb, m) for m in range(2, 8)]
        c_ops.append(("LE", NTB - 1, 0))
        sqF, leF = {}, {}
        n = s_sc.n
        for op in c_ops:
            n += 1
            if op[0] == "SQ":
                sqF[(op[1], op[2])] = n
            else:
                leF[op[1]] = n
        ev3 = [n + st + 1 for st in range(3)]

        obF, trF, stS = {}, {}, {}
        n = s_dma.n
        for tb in range(NTB):
            for m in range(8):
                n += 16
                obF[(tb, m)] = n
            if tb < NTB - 1:
                for c in range(4):
                    n += 16
                    trF[(tb, c)] = n
        n += 16
        trF[(NTB - 1, 3)] = n
        for st in range(3):
            n += 16
            stS[st] = n
        ocF = {}
        n2 = s_cast.n
        for tb in range(NTB - 1):
            for c in range(4):
                n2 += 16
                ocF[(tb, c)] = n2
        n2 += 16
        ocF[(NTB - 1, 3)] = n2
        trb = {}

        if True:
            @blk.sync
            def _(s):
                for tb in range(NTB - 1):
                    for m in range(8):
                        s.wait_ge(s_ve.sem, nmF[(tb, m)])
                        s_dma.dma2(s, lambda: s.dma_start(
                            out=outb[m * P:(m + 1) * P, tb * TB:(tb + 1) * TB],
                            in_=ofb[:, m % 4, :]))
                    for mm_ in range(8):
                        s_dma.wait(s, obF[(tb, mm_)])
                    for c in range(4):
                        if c >= 2:
                            s_cast.wait(s, ocF[(tb, c - 2)])
                        elif tb >= 1:
                            s_cast.wait(s, ocF[(tb - 1, c + 2)])
                        s_dma.dma2(s, lambda: s.dma_start_transpose(
                            otm[:, c % 2, :], outb[:, (tb * 4 + c) * P:(tb * 4 + c + 1) * P]))
                tb = NTB - 1
                for m in range(8):
                    s.wait_ge(s_ve.sem, nmF[(tb, m)])
                    s_dma.dma2(s, lambda: s.dma_start(
                        out=outb[m * P:(m + 1) * P, tb * TB + 3 * P:(tb + 1) * TB],
                        in_=ofb[:, m % 4, 3 * P:TB]))
                for mm_ in range(8):
                    s_dma.wait(s, obF[(tb, mm_)])
                s.wait_ge(s_mm.sem, stF[(tb, 7)])
                s_dma.dma2(s, lambda: s.dma_start_transpose(
                    ot3[:, 0, :], outb[:, (tb * 4 + 3) * P:(tb * 4 + 4) * P]))
                for st in range(3):
                    s.wait_ge(s_sc.sem, ev3[st])
                    s_dma.dma2(s, lambda: s.dma_start(
                        out=out[(tb * 4 + st) * P:(tb * 4 + st + 1) * P, :],
                        in_=otf[:, 0, :]))
                    s_dma.wait(s, stS[st])

            @blk.gpsimd
            def _(g):
                for tb in range(NTB - 1):
                    for c in range(4):
                        s_dma.wait(g, trF[(tb, c)])
                        s_cast.dma2(g, lambda: g.dma_start(
                            out=out[(tb * 4 + c) * P:(tb * 4 + c + 1) * P, :],
                            in_=otm[:, c % 2, :]))
                s_dma.wait(g, trF[(NTB - 1, 3)])
                s_cast.dma2(g, lambda: g.dma_start(
                    out=out[(NTB * 4 - 1) * P:NTB * 4 * P, :], in_=ot3[:, 0, :]))
                s_cast.wait_all(g)

            @blk.tensor
            def _(t):
                for op in t_ops:
                    kind, tb, m = op
                    gm = tb * 8 + m
                    if kind == "G":
                        par = gm % 2
                        hh, mh = divmod(m, 4)
                        if gm == 0:
                            t.wait_ge(s_sc.sem, seamD["siluD"][23])
                        if gm >= 2:
                            t.wait_ge(s_ve.sem, evF[divmod(gm - 2, 8)])
                        ins = None
                        for k in range(HC):
                            if gm == 0:
                                t.wait_ge(s_ve.sem, seamD["hdone"][k])
                                s_dma.wait(t, seamD["wd_str"][0][k])
                            elif gm < 8 and mh == 0:
                                s_dma.wait(t, seamD["wd_str"][hh][k])
                            ins = t.matmul(
                                pmm[par],
                                wdh[hh][:, k, mh * P:(mh + 1) * P],
                                h_fm[:, k, tb * TB:(tb + 1) * TB],
                                start=(k == 0), stop=(k == HC - 1),
                            )
                        ins.then_inc(s_mm.sem, 1)
                    elif kind == "S":
                        t.wait_ge(s_ve.sem, evF[(tb, m)])
                        t.wait_ge(s_sc.sem, sqF[(tb, m)])
                        if m == 0 and tb >= 1:
                            t.wait_ge(s_ve.sem, vvF[tb - 1])
                        t.matmul(pS, onesb2[:, 0:1], r2f[:, slot(tb, m), :],
                                 start=(m == 0), stop=(m == 7))
                        ins = t.matmul(pQ, onesb2[:, 0:1], sq2[:, m % 2, :],
                                       start=(m == 0), stop=(m == 7))
                        ins.then_inc(s_mm.sem, 1)
                    else:
                        t.wait_ge(s_sc.sem, leF[tb])
                        if tb >= 1:
                            t.wait_ge(s_ve.sem, nmF[(tb - 1, 7)])
                        t.matmul(pMB, onesr2[:, :], meant2[:, :], start=True, stop=True)
                        ins = t.matmul(pRB, onesr2[:, :], rstd2[:, :], start=True, stop=True)
                        ins.then_inc(s_mm.sem, 1)
                tb3 = NTB - 1
                nmm = max(list(bcF.values()) + list(stF.values())
                          + list(mmF.values()))
                for m in range(8):
                    trb[m] = nmm + m + 1
                    t.wait_ge(s_ve.sem, nmF[(tb3, m)])
                    ins = None
                    for st in range(3):
                        ins = t.matmul(
                            strip[st][:, m * P:(m + 1) * P],
                            ofb[:, m % 4, st * P:(st + 1) * P], ident[:],
                            start=True, stop=True)
                    ins.then_inc(s_mm.sem, 1)

            @blk.vector
            def _(v):
                v.memset(onesb2[:], 1.0)
                ins = v.memset(onesr2[:], 1.0)
                ins.then_inc(s_ve.sem, 1)
                for op in v_ops:
                    kind, tb, x = op
                    if kind == "EV":
                        gm = tb * 8 + x
                        v.wait_ge(s_mm.sem, mmF[(tb, x)])
                        ins = v.scalar_tensor_tensor(
                            r2f[:, slot(tb, x), :], pmm[gm % 2], SCALE,
                            pl_fm[:, x, tb * TB:(tb + 1) * TB],
                            ALU.mult, ALU.add)
                        ins.then_inc(s_ve.sem, 1)
                    elif kind == "MC":
                        v.wait_ge(s_mm.sem, stF[(tb, 7)])
                        v.tensor_scalar(meant2[:], pS, 1.0 / D, None, ALU.mult)
                        v.tensor_mul(vart2[:], meant2[:], meant2[:])
                        ins = v.scalar_tensor_tensor(vart2[:], pQ, 1.0 / D,
                                                     vart2[:], ALU.mult, ALU.subtract)
                        ins.then_inc(s_ve.sem, 1)
                    elif kind == "NM":
                        m = x
                        if m == 0:
                            v.wait_ge(s_mm.sem, bcF[tb])
                        if (tb, m - 4) in obF:
                            s_dma.wait(v, obF[(tb, m - 4)])
                            if tb == NTB - 1:
                                v.wait_ge(s_mm.sem, trb[m - 4])
                        elif tb >= 1 and m < 4:
                            s_dma.wait(v, obF[(tb - 1, m + 4)])
                        v.tensor_sub(ofb[:, m % 4, :], r2f[:, slot(tb, m), :], pMB)
                        ins = v.tensor_mul(ofb[:, m % 4, :], ofb[:, m % 4, :], pRB)
                        ins.then_inc(s_ve.sem, 1)


            @blk.scalar
            def _(sc):
                for op in c_ops:
                    kind, tb, m = op
                    if kind == "SQ":
                        sc.wait_ge(s_ve.sem, evF[(tb, m)])
                        if (tb, m - 2) in stF and (tb * 8 + m) >= 2:
                            sc.wait_ge(s_mm.sem, stF[(tb, m - 2)])
                        elif tb >= 1 and m < 2:
                            sc.wait_ge(s_mm.sem, stF[(tb - 1, m + 6)])
                        ins = sc.activation(sq2[:, m % 2, :],
                                            r2f[:, slot(tb, m), :], AF.Square)
                        ins.then_inc(s_sc.sem, 1)
                    else:
                        sc.wait_ge(s_ve.sem, vvF[tb])
                        sc.activation(vart2[:], vart2[:], AF.Ln, bias=EPS)
                        ins = sc.activation(rstd2[:], vart2[:], AF.Exp, scale=-0.5)
                        ins.then_inc(s_sc.sem, 1)
                for st in range(3):
                    if st == 0:
                        sc.wait_ge(s_mm.sem, trb[7])
                        s_cast.wait(sc, ocF[(NTB - 2, 3)])
                    else:
                        s_dma.wait(sc, stS[st - 1])
                    ins = sc.activation(otf[:, 0, :], strip[st][:, 0:D],
                                        AF.Copy)
                    ins.then_inc(s_sc.sem, 1)

        s_mm.n = trb[7]
        s_ve.n = nmF[(NTB - 1, 7)]
        s_sc.n = leF[NTB - 1] + 3
        s_dma.n = stS[2]
        s_cast.n = ocF[(NTB - 1, 3)]


_NC_CACHE = None


def kernel(L0_attn_x, L0_attn_W_in, L0_attn_W_out, L0_attn_conv_w, L0_attn_conv_b,
           L0_attn_dt_w, L0_attn_dt_b, L0_attn_A, L0_attn_Dp,
           L0_ffn_w_gate, L0_ffn_w_up, L0_ffn_w_down):
    global _NC_CACHE
    x = np.ascontiguousarray(np.asarray(L0_attn_x, dtype=np.float32))
    com = dict(
        w_in=np.ascontiguousarray(np.asarray(L0_attn_W_in, dtype=np.float32)),
        w_out=np.ascontiguousarray(np.asarray(L0_attn_W_out, dtype=np.float32)),
        conv_w=np.ascontiguousarray(np.asarray(L0_attn_conv_w, dtype=np.float32)),
        conv_b=np.ascontiguousarray(np.asarray(L0_attn_conv_b, dtype=np.float32)),
        dt_w=np.ascontiguousarray(np.asarray(L0_attn_dt_w, dtype=np.float32)),
        dt_b=np.ascontiguousarray(np.asarray(L0_attn_dt_b, dtype=np.float32)),
        A=np.ascontiguousarray(np.asarray(L0_attn_A, dtype=np.float32)),
        Dp=np.ascontiguousarray(np.asarray(L0_attn_Dp, dtype=np.float32)),
        wg=np.ascontiguousarray(np.asarray(L0_ffn_w_gate, dtype=np.int32)),
        wu=np.ascontiguousarray(np.asarray(L0_ffn_w_up, dtype=np.int32)),
        wd=np.ascontiguousarray(np.asarray(L0_ffn_w_down, dtype=np.int32)),
    )
    if _NC_CACHE is None:
        _NC_CACHE = build()
    in_maps = [dict(x=np.ascontiguousarray(x[b]), **com) for b in range(8)]
    res = run_bass_kernel_spmd(_NC_CACHE, in_maps, core_ids=list(range(8)))
    return np.stack([np.asarray(r["out"]) for r in res.results], axis=0)


# version 55
# speedup vs baseline: 1.0687x; 1.0687x over previous
import contextlib
import numpy as np
import concourse.bass as bass
import concourse.mybir as mybir
from concourse.bass_utils import run_bass_kernel_spmd

F32 = mybir.dt.float32
BF16 = mybir.dt.bfloat16
I32 = mybir.dt.int32
AF = mybir.ActivationFunctionType
ALU = mybir.AluOpType

D, DC = 1024, 8
CI, CC = 1536, 12
H, HC = 3072, 24
L, TC = 2048, 16
NT, TTW = 4, 512
SCALE = 1.0 / 127.0
EPS = 1e-5
P = 128


class S:

    def __init__(self, nc, name):
        self.sem = nc.alloc_semaphore(name)
        self.n = 0

    def dma(self, inst):
        inst.then_inc(self.sem, 16)
        self.n += 16
        return self.n

    def inc(self, inst):
        inst.then_inc(self.sem, 1)
        self.n += 1
        return self.n


class RingS:

    def __init__(self, nc, name, depth):
        self.sems = [nc.alloc_semaphore(f"{name}{j}") for j in range(depth)]
        self.depth = depth
        self.n = 0

    def dma2(self, eng, fn):
        i = self.n // 16
        if i >= self.depth:
            eng.wait_ge(self.sems[i % self.depth], 16 * (i // self.depth))
        ins = fn()
        ins.then_inc(self.sems[i % self.depth], 16)
        self.n += 16
        return self.n

    def wait(self, eng, val):
        if val <= 0:
            return
        i = val // 16 - 1
        eng.wait_ge(self.sems[i % self.depth], 16 * (i // self.depth + 1))

    def wait_all(self, eng):
        total = self.n // 16
        for j in range(min(self.depth, total)):
            i = total - 1 - j
            eng.wait_ge(self.sems[i % self.depth], 16 * (i // self.depth + 1))


def build():
    nc = bass.Bass("TRN2", target_bir_lowering=False)

    x_in = nc.dram_tensor("x", [L, D], F32, kind="ExternalInput").ap()
    w_in = nc.dram_tensor("w_in", [H, D], F32, kind="ExternalInput").ap()
    w_out = nc.dram_tensor("w_out", [D, CI], F32, kind="ExternalInput").ap()
    conv_w = nc.dram_tensor("conv_w", [CI, 1, 3], F32, kind="ExternalInput").ap()
    conv_b = nc.dram_tensor("conv_b", [CI], F32, kind="ExternalInput").ap()
    dt_w = nc.dram_tensor("dt_w", [CI, CI], F32, kind="ExternalInput").ap()
    dt_b = nc.dram_tensor("dt_b", [CI], F32, kind="ExternalInput").ap()
    A_in = nc.dram_tensor("A", [CI], F32, kind="ExternalInput").ap()
    Dp_in = nc.dram_tensor("Dp", [CI], F32, kind="ExternalInput").ap()
    wg = nc.dram_tensor("wg", [H, D], I32, kind="ExternalInput").ap()
    wu = nc.dram_tensor("wu", [H, D], I32, kind="ExternalInput").ap()
    wd = nc.dram_tensor("wd", [D, H], I32, kind="ExternalInput").ap()
    out = nc.dram_tensor("out", [L, D], F32, kind="ExternalOutput").ap()

    xbf = nc.dram_tensor("xbf", [L, D], BF16).ap()
    wib = nc.dram_tensor("wib", [H, D], BF16).ap()
    dtwb = nc.dram_tensor("dtwb", [CI, CI], BF16).ap()
    wob = nc.dram_tensor("wob", [D, CI], BF16).ap()
    wgb = nc.dram_tensor("wgb", [H, D], BF16).ap()
    wub = nc.dram_tensor("wub", [H, D], BF16).ap()
    wdb = nc.dram_tensor("wdb", [D, H], BF16).ap()
    attnb = nc.dram_tensor("attnb", [D, L], BF16).ap()
    postb = nc.dram_tensor("postb", [D, L], BF16).ap()
    ffnb = nc.dram_tensor("ffnb", [D, L], BF16).ap()

    s_cast = RingS(nc, "s_cast", 4)
    s_dma = RingS(nc, "s_dma", 8)
    s_mm = S(nc, "s_mm")
    s_sc = S(nc, "s_sc")
    s_ve = S(nc, "s_ve")

    top = contextlib.ExitStack()
    with top:
        sb = lambda name, shape, dt: top.enter_context(nc.sbuf_tensor(name, shape, dt))
        cwk = sb("cwk", [P, 3, CC], F32)
        cbv = sb("cbv", [P, CC], F32)
        dbv = sb("dbv", [P, CC], F32)
        Av = sb("Av", [P, CC], F32)
        Dpv = sb("Dpv", [P, CC], F32)
        ones = sb("ones", [P, 1], F32)
        stats = sb("stats", [P, 2, 3], F32)
        cst = sb("cst", [P, 5], F32)
        ident = sb("ident", [P, P], BF16)
        zt = sb("zt", [P, P + 1], BF16)
        idd = nc.dram_tensor("idd", [P, P + 1], BF16).ap()
        cvals = [EPS, SCALE, -0.5, 1.0 / D, -1.0 / D]
        for i, vv in enumerate(cvals):
            nc.const_aps.register(F32, vv, cst[:, i:i + 1])
        psA = top.enter_context(nc.psum_tensor("psA", [P, L], F32))
        psB = top.enter_context(nc.psum_tensor("psB", [P, L], F32))
        ps = [psA, psB]

        _prologue(nc, s_cast, s_dma, x_in, w_in, dt_w, w_out, wg, wu, wd,
                  xbf, wib, dtwb, wob, wgb, wub, wdb,
                  conv_w, conv_b, dt_b, A_in, Dp_in, cwk, cbv, dbv, Av, Dpv, ones,
                  cst, cvals, ident, zt, idd)
        m_cast = dict(wibL=16, wibR=32, wib2=48, dtwb=64, wob=80, wgb=96,
                      wub=112, wdb=128)

        pl_stack = contextlib.ExitStack()
        mid = contextlib.ExitStack()
        blkstack = contextlib.ExitStack()
        blk = blkstack.enter_context(nc.Block())
        x_fm = pl_stack.enter_context(nc.sbuf_tensor("x_fm", [P, DC, L], BF16))
        with mid:
            yz = mid.enter_context(nc.sbuf_tensor("yz", [P, CC, L], BF16))
            ab = contextlib.ExitStack()
            with ab:
                x_conv = ab.enter_context(nc.sbuf_tensor("x_conv", [P, CC, L], BF16))
                casts = [(wib, w_in), (dtwb, dt_w), (wob, w_out),
                         (wgb, wg), (wub, wu), (wdb, wd)]
                mmA, scA = _phase_a(nc, blk, s_cast, s_dma, s_mm, s_sc, s_ve,
                                    m_cast, x_in, wib, x_conv, yz, cwk, cbv,
                                    ps, casts, x_fm, ident)
                m_cast["_m_dtwT"] = s_dma.n + 12 * 16
                wo_stack = contextlib.ExitStack()
                woT = wo_stack.enter_context(
                    nc.sbuf_tensor("woT", [P, CC, D], BF16, side="right"))
                m_woT, seamB = _phase_b(nc, blk, s_cast, s_dma, s_mm, s_sc,
                                        s_ve, m_cast, dtwb, x_conv, yz, dbv,
                                        Av, Dpv, ones, ps,
                                        [casts[4], casts[5]], woT, wob,
                                        mmA, scA)
            plnorm, seamC = _phase_c(nc, blk, s_cast, s_dma, s_mm, s_sc, s_ve,
                                     m_cast, wob, x_fm, yz, woT, m_woT,
                                     ones, cst, ps, seamB)
            wo_stack.close()

        with pl_stack:
            dstk = contextlib.ExitStack()
            with dstk:
                h_fm = dstk.enter_context(nc.sbuf_tensor("h_fm", [P, HC, L], BF16))
                wdh = [dstk.enter_context(nc.sbuf_tensor(f"wdh{i}", [P, HC, L // 4], BF16)) for i in range(2)]
                seamD = _phase_d1(nc, blk, s_cast, s_dma, s_mm, s_sc, s_ve, m_cast,
                                  plnorm, wgb, wub, x_fm, h_fm, psA, psB,
                                  wdb, wdh, seamB, seamC)
                _phase_d2e(nc, blk, s_cast, s_dma, s_mm, s_sc, s_ve, m_cast,
                           wdb, ffnb, h_fm, x_fm, out, ps, wdh, seamD, ident)
            blkstack.close()

    return nc


def _prologue(nc, s_cast, s_dma, x_in, w_in, dt_w, w_out, wg, wu, wd,
              xbf, wib, dtwb, wob, wgb, wub, wdb,
              conv_w, conv_b, dt_b, A_in, Dp_in, cwk, cbv, dbv, Av, Dpv, ones,
              cst, cvals, ident, zt, idd):
    pv = nc.alloc_semaphore("pv")
    with nc.Block() as blk:
        @blk.sync
        def _(s):
            kw = dict(allow_slow_non_contiguous=True)
            s_dma.dma2(s, lambda: s.dma_start(out=cwk[:], in_=conv_w.rearrange("(j p) o k -> p (o k) j", p=P), **kw))
            s_dma.dma2(s, lambda: s.dma_start(out=cbv[:], in_=conv_b.rearrange("(j p) -> p j", p=P), **kw))
            s_dma.dma2(s, lambda: s.dma_start(out=dbv[:], in_=dt_b.rearrange("(j p) -> p j", p=P), **kw))
            s_dma.dma2(s, lambda: s.dma_start(out=Av[:], in_=A_in.rearrange("(j p) -> p j", p=P), **kw))
            s_dma.dma2(s, lambda: s.dma_start(out=Dpv[:], in_=Dp_in.rearrange("(j p) -> p j", p=P), **kw))
            s.wait_ge(pv, 1)
            m1 = s_dma.dma2(s, lambda: s.dma_start(out=idd, in_=zt[:]))
            s_dma.wait(s, m1)
            iv = idd.rearrange("p q -> (p q)").rearrange("(a b) -> a b", b=P)
            s_dma.dma2(s, lambda: s.dma_start(out=ident[:], in_=iv[0:P, :]))

        @blk.vector
        def _(v):
            v.memset(ones[:], 1.0)
            for i, vv in enumerate(cvals):
                v.memset(cst[:, i:i + 1], vv)
            v.memset(zt[:, 1:P + 1], 0.0)
            ins = v.memset(zt[:, 0:1], 1.0)
            ins.then_inc(pv, 1)


def _phase_a(nc, blk, s_cast, s_dma, s_mm, s_sc, s_ve, m_cast, x_in, wib,
             x_conv, yz, cwk, cbv, ps, casts, x_fm, ident):
    stk = contextlib.ExitStack()
    with stk:
        wiT = stk.enter_context(nc.sbuf_tensor("wiT", [P, DC, CI], BF16))
        wiT2 = stk.enter_context(nc.sbuf_tensor("wiT2", [P, DC, CI], BF16))
        stg1a = stk.enter_context(nc.sbuf_tensor("stg1a", [P, L], F32))
        stg1b = stk.enter_context(nc.sbuf_tensor("stg1b", [P, L], F32))
        stg2 = stk.enter_context(nc.sbuf_tensor("stg2", [P, L], F32))
        stg1 = [stg1a, stg1b]
        xtm = nc.alloc_sbuf_tensor_at("xtm", (P, TC, D), BF16,
                                      offset=nc.lookup_mloc(yz).addr)
        xf32 = nc.alloc_sbuf_tensor_at("xf32", (P, 4, D), F32,
                                       offset=nc.lookup_mloc(stg1a).addr)

        xld = [s_dma.n + 16 * (c + 1) for c in range(TC)]
        trmm = [s_mm.n + c + 1 for c in range(TC)]
        cast_ms, evac_ms = [0] * TC, [0] * TC
        n = s_sc.n
        sc_plan = []
        for c in range(TC):
            sc_plan.append(("C", c))
            if c >= 3:
                sc_plan.append(("E", c - 3))
        sc_plan += [("E", c) for c in range(TC - 3, TC)]
        for op, c in sc_plan:
            n += 1
            (cast_ms if op == "C" else evac_ms)[c] = n
        mmA = [trmm[-1] + m + 1 for m in range(24)]
        veA = [s_ve.n + m + 1 for m in range(12)]
        scA = [n + m + 1 for m in range(24)]
        trA = [xld[-1] + 16 * (k + 1) for k in range(DC)]
        m_wiT1 = trA[-1]
        m_wiT2 = m_wiT1 + 8 * 16
        m_cast["_m_dtwT_plan"] = m_wiT2 + 12 * 16

        if True:
            @blk.gpsimd
            def _(g):
                (wib_d, wsrc) = casts[0]
                s_cast.dma2(g, lambda: g.dma_start(out=wib_d[0:CI, 0:D // 2], in_=wsrc[0:CI, 0:D // 2]))
                s_cast.dma2(g, lambda: g.dma_start(out=wib_d[0:CI, D // 2:D], in_=wsrc[0:CI, D // 2:D]))
                s_dma.wait(g, m_wiT1)
                s_cast.dma2(g, lambda: g.dma_start(out=wib_d[CI:H, :], in_=wsrc[CI:H, :]))
                s_dma.wait(g, m_wiT2)
                s_cast.dma2(g, lambda: g.dma_start(out=casts[1][0], in_=casts[1][1]))
                s_cast.dma2(g, lambda: g.dma_start(out=casts[2][0], in_=casts[2][1]))
                s_cast.dma2(g, lambda: g.dma_start(out=casts[3][0], in_=casts[3][1]))

            @blk.sync
            def _(s):
                for c in range(TC):
                    if c >= 4:
                        s.wait_ge(s_sc.sem, cast_ms[c - 4])
                    s_dma.dma2(s, lambda: s.dma_start(
                        out=xf32[:, c % 4, :], in_=x_in[c * P:(c + 1) * P, :]))
                for k in range(DC):
                    s_cast.wait(s, m_cast["wibL" if k < DC // 2 else "wibR"])
                    s_dma.dma2(s, lambda: s.dma_start_transpose(wiT[:, k, :], wib[0:CI, k * P:(k + 1) * P]))
                s_cast.wait(s, m_cast["wib2"])
                for k in range(DC):
                    s_dma.dma2(s, lambda: s.dma_start_transpose(wiT2[:, k, :], wib[CI:H, k * P:(k + 1) * P]))

            @blk.tensor
            def _(t):
                for c in range(TC):
                    t.wait_ge(s_sc.sem, cast_ms[c])
                    if c >= 2:
                        t.wait_ge(s_sc.sem, evac_ms[c - 2])
                    base = (c % 2) * (L // 2)
                    ins = None
                    for k in range(DC):
                        ins = t.matmul(
                            ps[0][:, base + k * P:base + (k + 1) * P],
                            xtm[:, c, k * P:(k + 1) * P], ident[:],
                            start=True, stop=True)
                    ins.then_inc(s_mm.sem, 1)
                for m in range(24):
                    mm = m % 12
                    pp = m % 2
                    wt = wiT if m < 12 else wiT2
                    if m == 0:
                        t.wait_ge(s_sc.sem, evac_ms[-1])
                    if 2 <= m < 14:
                        t.wait_ge(s_ve.sem, veA[m - 2])
                    elif m >= 14:
                        t.wait_ge(s_sc.sem, scA[m - 2])
                    ins = None
                    for k in range(DC):
                        if m == 0:
                            s_dma.wait(t, trA[k])
                        elif m == 12:
                            s_dma.wait(t, m_wiT1 + (k + 1) * 16)
                        for tt in range(NT):
                            ins = t.matmul(
                                ps[pp][:, tt * TTW:(tt + 1) * TTW],
                                wt[:, k, mm * P:(mm + 1) * P],
                                x_fm[:, k, tt * TTW:(tt + 1) * TTW],
                                start=(k == 0), stop=(k == DC - 1),
                            )
                    ins.then_inc(s_mm.sem, 1)

            @blk.vector
            def _(v):
                for m in range(12):
                    pp = m % 2
                    if m >= 2:
                        v.wait_ge(s_sc.sem, scA[m - 2])
                    v.wait_ge(s_mm.sem, mmA[m])
                    v.tensor_scalar(stg1[pp][:, 1:L], ps[pp][:, 0:L - 1],
                                    cwk[:, 0, m:m + 1], None, ALU.mult)
                    v.tensor_scalar(stg2[:, 0:1], ps[pp][:, 0:1],
                                    cwk[:, 1, m:m + 1], None, ALU.mult)
                    v.scalar_tensor_tensor(stg2[:, 1:L], ps[pp][:, 1:L],
                                           cwk[:, 1, m:m + 1],
                                           stg1[pp][:, 1:L], ALU.mult, ALU.add)
                    v.scalar_tensor_tensor(stg1[pp][:, 0:L - 1], ps[pp][:, 1:L],
                                           cwk[:, 2, m:m + 1], stg2[:, 0:L - 1],
                                           ALU.mult, ALU.add)
                    ins = v.tensor_copy(stg1[pp][:, L - 1:L], stg2[:, L - 1:L])
                    ins.then_inc(s_ve.sem, 1)

            @blk.scalar
            def _(sc):
                for op, c in sc_plan:
                    if op == "C":
                        s_dma.wait(sc, xld[c])
                        ins = sc.activation(xtm[:, c, :], xf32[:, c % 4, :],
                                            AF.Copy)
                    else:
                        sc.wait_ge(s_mm.sem, trmm[c])
                        base = (c % 2) * (L // 2)
                        for k in range(DC):
                            ins = sc.activation(
                                x_fm[:, k, c * P:(c + 1) * P],
                                ps[0][:, base + k * P:base + (k + 1) * P],
                                AF.Copy)
                    ins.then_inc(s_sc.sem, 1)
                for m in range(12):
                    pp = m % 2
                    sc.wait_ge(s_ve.sem, veA[m])
                    ins = sc.activation(x_conv[:, m, :], stg1[pp][:], AF.Silu,
                                        bias=cbv[:, m:m + 1])
                    ins.then_inc(s_sc.sem, 1)
                for m in range(12, 24):
                    pp = m % 2
                    sc.wait_ge(s_mm.sem, mmA[m])
                    ins = sc.activation(yz[:, m - 12, :], ps[pp][:, :], AF.Silu)
                    ins.then_inc(s_sc.sem, 1)

        s_mm.n = mmA[-1]
        s_ve.n = veA[-1]
        s_sc.n = scA[-1]
        s_dma.n = m_wiT2
        return mmA, scA


def _phase_b(nc, blk, s_cast, s_dma, s_mm, s_sc, s_ve, m_cast, dtwb,
             x_conv, yz, dbv, Av, Dpv, ones, ps, late_casts, woT, wob,
             mmA, scA):
    psA, psB = ps
    HF = L // 2
    stk = contextlib.ExitStack()
    with stk:
        dtwT = stk.enter_context(nc.sbuf_tensor("dtwT", [P, CC, CI], BF16))
        bsA = [stk.enter_context(nc.sbuf_tensor(f"bsA{i}", [P, HF], BF16)) for i in range(2)]
        bsB = [stk.enter_context(nc.sbuf_tensor(f"bsB{i}", [P, HF], BF16)) for i in range(2)]
        bsC = [stk.enter_context(nc.sbuf_tensor(f"bsC{i}", [P, HF], BF16)) for i in range(2)]
        carryv = stk.enter_context(nc.sbuf_tensor("carryv", [P, CC], F32))

        pr = [psA[:, 0:HF], psA[:, HF:L], psB[:, 0:HF], psB[:, HF:L]]

        mmB = [s_mm.n + g + 1 for g in range(24)]
        pfree = [s_sc.n + 2 * g + 1 for g in range(24)]
        ready = [s_sc.n + 2 * g + 2 for g in range(24)]
        veB = [s_ve.n + g + 1 for g in range(24)]
        m_dtwT = [s_dma.n + (k + 1) * 16 for k in range(CC)]
        m_woT = [m_dtwT[-1] + (k + 1) * 16 for k in range(CC)]
        ones_hb = ones[:, 0:1].broadcast_to((P, HF))

        if True:
            @blk.gpsimd
            def _(g):
                s_dma.wait(g, m_woT[-1])
                for dst, srcp in late_casts:
                    s_cast.dma2(g, lambda: g.dma_start(out=dst, in_=srcp))

            @blk.sync
            def _(s):
                s_cast.wait(s, m_cast["dtwb"])
                s.wait_ge(s_mm.sem, mmA[23])
                for k in range(CC):
                    s_dma.dma2(s, lambda: s.dma_start_transpose(
                        dtwT[:, k, :], dtwb[0:CI, k * P:(k + 1) * P]))
                s_cast.wait(s, m_cast["wob"])
                for k in range(CC):
                    s_dma.dma2(s, lambda: s.dma_start_transpose(
                        woT[:, k, :], wob[0:D, k * P:(k + 1) * P]))

            bpar = lambda g: (g % 4) if g < 20 else 2 + (g % 2)

            @blk.tensor
            def _(t):
                for g in range(24):
                    h, m = divmod(g, 12)
                    par = bpar(g)
                    if g < 2:
                        t.wait_ge(s_sc.sem, scA[22])
                    elif g < 4:
                        t.wait_ge(s_sc.sem, scA[23])
                    elif g < 20:
                        t.wait_ge(s_sc.sem, pfree[g - 4])
                    else:
                        t.wait_ge(s_sc.sem, pfree[g - 2])
                    ins = None
                    for k in range(CC):
                        if g == 0:
                            t.wait_ge(s_sc.sem, scA[k])
                        if h == 0 and m == 0:
                            s_dma.wait(t, m_dtwT[k])
                        for tt in range(2):
                            ins = t.matmul(
                                pr[par][:, tt * TTW:(tt + 1) * TTW],
                                dtwT[:, k, m * P:(m + 1) * P],
                                x_conv[:, k, h * HF + tt * TTW:h * HF + (tt + 1) * TTW],
                                start=(k == 0), stop=(k == CC - 1),
                            )
                    ins.then_inc(s_mm.sem, 1)

            @blk.scalar
            def _(sc):
                for g in range(24):
                    h, m = divmod(g, 12)
                    par = bpar(g)
                    ss = g % 2
                    sc.wait_ge(s_mm.sem, mmB[g])
                    if g >= 2:
                        sc.wait_ge(s_ve.sem, veB[g - 2])
                    ins = sc.activation(bsA[ss][:], pr[par], AF.Exp,
                                        bias=dbv[:, m:m + 1])
                    ins.then_inc(s_sc.sem, 1)
                    sc.activation(bsB[ss][:], bsA[ss][:], AF.Ln, bias=1.0)
                    ins = sc.activation(bsC[ss][:], bsB[ss][:], AF.Exp,
                                        scale=Av[:, m:m + 1])
                    ins.then_inc(s_sc.sem, 1)

            @blk.vector
            def _(v):
                for g in range(24):
                    h, m = divmod(g, 12)
                    ss = g % 2
                    xs = x_conv[:, m, h * HF:(h + 1) * HF]
                    v.wait_ge(s_sc.sem, ready[g])
                    v.tensor_mul(bsA[ss][:], bsB[ss][:], xs)
                    v.tensor_mul(bsA[ss][:], bsA[ss][:], bsC[ss][:])
                    v.tensor_tensor_scan(bsB[ss][:], ones_hb, bsA[ss][:], 0.0,
                                         ALU.mult, ALU.add)
                    if h == 0:
                        v.tensor_scalar(carryv[:, m:m + 1],
                                        bsB[ss][:, HF - 1:HF], ones[:, 0:1],
                                        None, ALU.mult)
                    else:
                        v.tensor_scalar(bsB[ss][:], bsB[ss][:], carryv[:, m:m + 1],
                                        None, ALU.add)
                    v.tensor_mul(bsB[ss][:], bsB[ss][:], bsC[ss][:])
                    v.tensor_scalar(bsA[ss][:], xs, Dpv[:, m:m + 1], None, ALU.mult)
                    v.tensor_add(bsB[ss][:], bsB[ss][:], bsA[ss][:])
                    ins = v.tensor_mul(yz[:, m, h * HF:(h + 1) * HF], bsB[ss][:],
                                       yz[:, m, h * HF:(h + 1) * HF])
                    ins.then_inc(s_ve.sem, 1)

        s_mm.n = mmB[-1]
        s_ve.n = veB[-1]
        s_sc.n = ready[-1]
        s_dma.n = m_woT[-1]
        return m_woT, dict(mmB=mmB, veB=veB, pfree=pfree, ready=ready)


def _phase_c(nc, blk, s_cast, s_dma, s_mm, s_sc, s_ve, m_cast,
             wob, x_fm, yz, woT, m_woT, ones, cst, ps, seamB):
    psA, psB = ps
    HF = L // 2
    stk = contextlib.ExitStack()
    with stk:
        sq1 = stk.enter_context(nc.sbuf_tensor("sq1", [P, 2, HF], BF16))
        onesb = stk.enter_context(nc.sbuf_tensor("onesb", [P, 1], BF16))
        onesr = stk.enter_context(nc.sbuf_tensor("onesr", [1, P], BF16))
        vart = stk.enter_context(nc.sbuf_tensor("vart", [1, L], F32))
        meant = stk.enter_context(nc.sbuf_tensor("meant", [1, L], BF16))
        rstdt = stk.enter_context(nc.sbuf_tensor("rstdt", [1, L], BF16))
        mbc = stk.enter_context(nc.sbuf_tensor("mbc", [P, L], BF16))
        rbc = stk.enter_context(nc.sbuf_tensor("rbc", [P, L], BF16))

        pmm = [psA[:, 0:HF], psA[:, HF:L]]
        pS = psB[0:1, 0:HF]
        pQ = psB[0:1, HF:L]

        t_ops = []
        pend = []
        NG = 16
        for g in range(NG + 6):
            if g < NG:
                t_ops.append(("G",) + divmod(g, 8))
            for it in list(pend):
                if it[3] <= g:
                    t_ops.append(it[:3])
                    pend.remove(it)
            if g < NG:
                h, m = divmod(g, 8)
                pend.append(("S", h, m, g + 2))
                if g == NG - 1:
                    pend.append(("BC", 0, 0, g + 4))
        mmC, stC = {}, {}
        bcastC = [0]
        n = s_mm.n
        for op in t_ops:
            n += 1
            if op[0] == "G":
                mmC[(op[1], op[2])] = n
            elif op[0] == "S":
                stC[(op[1], op[2])] = n
            else:
                bcastC[0] = n
        bcastC = bcastC[0]

        v0 = s_ve.n + 1
        residC, mcC = {}, {}
        n = v0
        for g in range(NG):
            h, m = divmod(g, 8)
            n += 1
            residC[(h, m)] = n
            if m == 7:
                n += 1
                mcC[h] = n
        bcevC = n + 1
        normC = [bcevC + m + 1 for m in range(8)]
        sqC, leC = {}, {}
        n = s_sc.n
        for g in range(NG):
            h, m = divmod(g, 8)
            n += 1
            sqC[(h, m)] = n
            if m == 7:
                n += 1
                leC[h] = n
        if True:
            @blk.tensor
            def _(t):
                for op in t_ops:
                    kind, h, m = op
                    if kind == "G":
                        g = h * 8 + m
                        par = g % 2
                        if g < 2:
                            t.wait_ge(s_sc.sem, seamB["pfree"][16 + g])
                        else:
                            prev = divmod(g - 2, 8)
                            t.wait_ge(s_ve.sem, residC[prev])
                        ins = None
                        for k in range(CC):
                            if g == 0:
                                s_dma.wait(t, m_woT[k])
                            if m == 0:
                                t.wait_ge(s_ve.sem, seamB["veB"][h * 12 + k])
                            for tt in range(2):
                                ins = t.matmul(
                                    pmm[par][:, tt * TTW:(tt + 1) * TTW],
                                    woT[:, k, m * P:(m + 1) * P],
                                    yz[:, k, h * HF + tt * TTW:h * HF + (tt + 1) * TTW],
                                    start=(k == 0), stop=(k == CC - 1),
                                )
                        ins.then_inc(s_mm.sem, 1)
                    elif kind == "S":
                        g = h * 8 + m
                        t.wait_ge(s_ve.sem, residC[(h, m)])
                        t.wait_ge(s_sc.sem, sqC[(h, m)])
                        if h == 0 and m == 0:
                            t.wait_ge(s_sc.sem, seamB["pfree"][23])
                        if h == 1 and m == 0:
                            t.wait_ge(s_ve.sem, mcC[0])
                        ins = None
                        for tt in range(2):
                            t.matmul(pS[:, tt * TTW:(tt + 1) * TTW], onesb[:, 0:1],
                                     x_fm[:, m, h * HF + tt * TTW:h * HF + (tt + 1) * TTW],
                                     start=(m == 0), stop=(m == 7))
                            ins = t.matmul(pQ[:, tt * TTW:(tt + 1) * TTW], onesb[:, 0:1],
                                           sq1[:, g % 2, tt * TTW:(tt + 1) * TTW],
                                           start=(m == 0), stop=(m == 7))
                        ins.then_inc(s_mm.sem, 1)
                    else:
                        t.wait_ge(s_sc.sem, leC[1])
                        for tt in range(NT):
                            t.matmul(psA[:, tt * TTW:(tt + 1) * TTW], onesr[:, :],
                                     meant[:, tt * TTW:(tt + 1) * TTW], start=True, stop=True)
                        ins = None
                        for tt in range(NT):
                            ins = t.matmul(psB[:, tt * TTW:(tt + 1) * TTW], onesr[:, :],
                                           rstdt[:, tt * TTW:(tt + 1) * TTW], start=True, stop=True)
                        ins.then_inc(s_mm.sem, 1)

            @blk.vector
            def _(v):
                v.memset(onesb[:], 1.0)
                ins = v.memset(onesr[:], 1.0)
                ins.then_inc(s_ve.sem, 1)
                for g in range(NG):
                    h, m = divmod(g, 8)
                    par = g % 2
                    v.wait_ge(s_mm.sem, mmC[(h, m)])
                    ins = v.tensor_add(x_fm[:, m, h * HF:(h + 1) * HF], pmm[par],
                                       x_fm[:, m, h * HF:(h + 1) * HF])
                    ins.then_inc(s_ve.sem, 1)
                    if m == 7:
                        v.wait_ge(s_mm.sem, stC[(h, 7)])
                        v.tensor_scalar(meant[:, h * HF:(h + 1) * HF], pS, 1.0 / D,
                                        None, ALU.mult)
                        v.tensor_mul(vart[:, h * HF:(h + 1) * HF],
                                     meant[:, h * HF:(h + 1) * HF],
                                     meant[:, h * HF:(h + 1) * HF])
                        ins = v.scalar_tensor_tensor(vart[:, h * HF:(h + 1) * HF], pQ,
                                                     1.0 / D, vart[:, h * HF:(h + 1) * HF],
                                                     ALU.mult, ALU.subtract)
                        ins.then_inc(s_ve.sem, 1)
                v.wait_ge(s_mm.sem, bcastC)
                v.tensor_copy(mbc[:], psA[:, :])
                ins = v.tensor_copy(rbc[:], psB[:, :])
                ins.then_inc(s_ve.sem, 1)
                for m in range(8):
                    v.tensor_sub(x_fm[:, m, :], x_fm[:, m, :], mbc[:])
                    ins = v.tensor_mul(x_fm[:, m, :], x_fm[:, m, :], rbc[:])
                    ins.then_inc(s_ve.sem, 1)

            @blk.scalar
            def _(sc):
                for g in range(NG):
                    h, m = divmod(g, 8)
                    sc.wait_ge(s_ve.sem, residC[(h, m)])
                    if g >= 2:
                        sc.wait_ge(s_mm.sem, stC[divmod(g - 2, 8)])
                    ins = sc.activation(sq1[:, g % 2, :],
                                        x_fm[:, m, h * HF:(h + 1) * HF], AF.Square)
                    ins.then_inc(s_sc.sem, 1)
                    if m == 7:
                        sc.wait_ge(s_ve.sem, mcC[h])
                        sc.activation(vart[:, h * HF:(h + 1) * HF],
                                      vart[:, h * HF:(h + 1) * HF], AF.Ln, bias=EPS)
                        ins = sc.activation(rstdt[:, h * HF:(h + 1) * HF],
                                            vart[:, h * HF:(h + 1) * HF],
                                            AF.Exp, scale=-0.5)
                        ins.then_inc(s_sc.sem, 1)

        s_mm.n = bcastC
        s_ve.n = normC[-1]
        s_sc.n = leC[1]
        return normC, dict(bcastC=bcastC, bcevC=bcevC, leC=leC,
                           mmClast=mmC[(1, 7)])


def _phase_d1(nc, blk, s_cast, s_dma, s_mm, s_sc, s_ve, m_cast, plnorm,
              wgb, wub, pl_fm, h_fm, psA, psB, wdb, wdh, seamB, seamC):
    QN, QM = 8, 3
    QH = 384
    stk = contextlib.ExitStack()
    with stk:
        wgq = [stk.enter_context(nc.sbuf_tensor(f"wgq{i}", [P, DC, QH], BF16)) for i in range(2)]
        wuq = [stk.enter_context(nc.sbuf_tensor(f"wuq{i}", [P, DC, QH], BF16)) for i in range(2)]
        gst = stk.enter_context(nc.sbuf_tensor("gst", [P, L], BF16))

        gdone = [s_mm.n + 2 * m + 1 for m in range(24)]
        udone = [s_mm.n + 2 * m + 2 for m in range(24)]
        hdone = [s_ve.n + m + 1 for m in range(24)]
        siluD = [s_sc.n + m + 1 for m in range(24)]
        base = s_dma.n
        wg_str = [[base + q * 256 + (k + 1) * 16 for k in range(DC)] for q in range(QN)]
        wu_str = [[base + q * 256 + 128 + (k + 1) * 16 for k in range(DC)] for q in range(QN)]
        wd0 = base + QN * 256
        wd_str = [[wd0 + hh * HC * 16 + (k + 1) * 16 for k in range(HC)]
                  for hh in range(2)]

        if True:
            @blk.sync
            def _(s):
                s_cast.wait(s, m_cast["wub"])
                s.wait_ge(s_ve.sem, seamB["veB"][23])
                s.wait_ge(s_sc.sem, seamB["ready"][23])
                s.wait_ge(s_mm.sem, seamC["mmClast"])
                for q in range(QN):
                    if q >= 2:
                        s.wait_ge(s_mm.sem, udone[QM * q - QM - 1])
                    for k in range(DC):
                        s_dma.dma2(s, lambda: s.dma_start_transpose(
                            wgq[q % 2][:, k, :], wgb[q * QH:(q + 1) * QH, k * P:(k + 1) * P]))
                    for k in range(DC):
                        s_dma.dma2(s, lambda: s.dma_start_transpose(
                            wuq[q % 2][:, k, :], wub[q * QH:(q + 1) * QH, k * P:(k + 1) * P]))
                s_cast.wait(s, m_cast["wdb"])
                for hh in range(2):
                    for k in range(HC):
                        s_dma.dma2(s, lambda: s.dma_start_transpose(
                            wdh[hh][:, k, :],
                            wdb[hh * (D // 2):(hh + 1) * (D // 2), k * P:(k + 1) * P]))

            @blk.tensor
            def _(t):
                for m in range(24):
                    q, mq = m // QM, m % QM
                    par = q % 2
                    if m >= 1:
                        t.wait_ge(s_sc.sem, siluD[m - 1])
                    ins = None
                    for k in range(DC):
                        if mq == 0:
                            s_dma.wait(t, wg_str[q][k])
                            if m == 0:
                                t.wait_ge(s_ve.sem, plnorm[k])
                        for tt in range(NT):
                            ins = t.matmul(
                                psA[:, tt * TTW:(tt + 1) * TTW],
                                wgq[par][:, k, mq * P:(mq + 1) * P],
                                pl_fm[:, k, tt * TTW:(tt + 1) * TTW],
                                start=(k == 0), stop=(k == DC - 1),
                            )
                    ins.then_inc(s_mm.sem, 1)
                    if m >= 1:
                        t.wait_ge(s_ve.sem, hdone[m - 1])
                    for k in range(DC):
                        if mq == 0:
                            s_dma.wait(t, wu_str[q][k])
                        for tt in range(NT):
                            ins = t.matmul(
                                psB[:, tt * TTW:(tt + 1) * TTW],
                                wuq[par][:, k, mq * P:(mq + 1) * P],
                                pl_fm[:, k, tt * TTW:(tt + 1) * TTW],
                                start=(k == 0), stop=(k == DC - 1),
                            )
                    ins.then_inc(s_mm.sem, 1)

            @blk.scalar
            def _(sc):
                for m in range(24):
                    sc.wait_ge(s_mm.sem, gdone[m])
                    if m >= 1:
                        sc.wait_ge(s_ve.sem, hdone[m - 1])
                    ins = sc.activation(gst[:], psA[:, :], AF.Silu,
                                        scale=SCALE)
                    ins.then_inc(s_sc.sem, 1)

            @blk.vector
            def _(v):
                for m in range(24):
                    v.wait_ge(s_sc.sem, siluD[m])
                    v.wait_ge(s_mm.sem, udone[m])
                    ins = v.scalar_tensor_tensor(h_fm[:, m, :], psB[:, :],
                                                 SCALE, gst[:],
                                                 ALU.mult, ALU.mult)
                    ins.then_inc(s_ve.sem, 1)

        s_mm.n = udone[-1]
        s_ve.n = hdone[-1]
        s_sc.n = siluD[-1]
        s_dma.n = wd_str[-1][-1]
        return dict(hdone=hdone, siluD=siluD, udone=udone, wd_str=wd_str)


def _phase_d2e(nc, blk, s_cast, s_dma, s_mm, s_sc, s_ve, m_cast,
               wdb, outb, h_fm, pl_fm, out, ps, wdh, seamD, ident):
    psA, psB = ps
    NTB = 4
    TB = L // NTB
    NSL = 12
    stk = contextlib.ExitStack()
    with stk:
        r2f = stk.enter_context(nc.sbuf_tensor("r2f", [P, NSL, TB], BF16))
        sq2 = stk.enter_context(nc.sbuf_tensor("sq2", [P, 2, TB], BF16))
        ofb = stk.enter_context(nc.sbuf_tensor("ofb", [P, 4, TB], BF16))
        otm = stk.enter_context(nc.sbuf_tensor("otm", [P, 2, D], BF16))
        onesb2 = stk.enter_context(nc.sbuf_tensor("onesb2", [P, 1], BF16))
        onesr2 = stk.enter_context(nc.sbuf_tensor("onesr2", [1, P], BF16))
        meant2 = stk.enter_context(nc.sbuf_tensor("meant2", [1, TB], BF16))
        rstd2 = stk.enter_context(nc.sbuf_tensor("rstd2", [1, TB], BF16))
        vart2 = stk.enter_context(nc.sbuf_tensor("vart2", [1, TB], F32))
        otf = nc.alloc_sbuf_tensor_at("otf", (P, 1, D), F32,
                                      offset=nc.lookup_mloc(otm).addr)
        ot3 = nc.alloc_sbuf_tensor_at("ot3", (P, 1, D), BF16,
                                      offset=nc.lookup_mloc(sq2).addr)

        pmm = [psA[:, 0:TB], psA[:, TB:2 * TB]]
        strip = [psA[:, 0:2 * TB], psB[:, 0:2 * TB], psB[:, 2 * TB:4 * TB]]
        pMB = psA[:, 2 * TB:3 * TB]
        pRB = psA[:, 3 * TB:4 * TB]
        pS = psB[0:1, 0:TB]
        pQ = psB[0:1, TB:2 * TB]
        slot = lambda tb, m: (tb * 8 + m) % NSL

        t_ops = []
        pend = []
        for gm in range(NTB * 8 + 6):
            if gm < NTB * 8:
                t_ops.append(("G",) + divmod(gm, 8))
            for it in list(pend):
                if it[3] <= gm:
                    t_ops.append(it[:3])
                    pend.remove(it)
            tb, m = divmod(gm, 8)
            if gm < NTB * 8:
                pend.append(("S", tb, m, gm + (3 if m == 0 else 2)))
                if m == 7:
                    pend.append(("BC", tb, 0, gm + 4))
        mmF, stF, bcF = {}, {}, {}
        n = s_mm.n
        for op in t_ops:
            n += 1
            if op[0] == "G":
                mmF[(op[1], op[2])] = n
            elif op[0] == "S":
                stF[(op[1], op[2])] = n
            else:
                bcF[op[1]] = n

        v_ops = []
        for tb in range(NTB):
            if tb == 0:
                v_ops += [("EV", 0, m) for m in range(8)]
                v_ops.append(("MC", 0, 0))
            else:
                v_ops += [("EV", tb, 0), ("EV", tb, 1), ("EV", tb, 2),
                          ("NM", tb - 1, 0), ("NM", tb - 1, 1), ("EV", tb, 3),
                          ("NM", tb - 1, 2), ("NM", tb - 1, 3), ("EV", tb, 4),
                          ("NM", tb - 1, 4), ("NM", tb - 1, 5), ("EV", tb, 5),
                          ("EV", tb, 6),
                          ("NM", tb - 1, 6), ("NM", tb - 1, 7), ("EV", tb, 7)]
                v_ops.append(("MC", tb, 0))
        v_ops += [("NM", NTB - 1, m) for m in range(8)]
        evF, vvF, nmF = {}, {}, {}
        n = s_ve.n + 1
        for op in v_ops:
            n += 1
            d = {"EV": evF, "NM": nmF}.get(op[0])
            if d is not None:
                d[(op[1], op[2])] = n
            else:
                vvF[op[1]] = n

        c_ops = []
        for tb in range(NTB):
            if tb == 0:
                c_ops += [("SQ", 0, m) for m in range(8)]
            else:
                c_ops += [("SQ", tb, 0), ("SQ", tb, 1)]
                c_ops.append(("LE", tb - 1, 0))
                c_ops += [("SQ", tb, m) for m in range(2, 8)]
        c_ops.append(("LE", NTB - 1, 0))
        sqF, leF = {}, {}
        n = s_sc.n
        for op in c_ops:
            n += 1
            if op[0] == "SQ":
                sqF[(op[1], op[2])] = n
            else:
                leF[op[1]] = n
        ev3 = [n + st + 1 for st in range(3)]

        obF, trF, stS = {}, {}, {}
        n = s_dma.n
        for tb in range(NTB):
            for m in range(8):
                n += 16
                obF[(tb, m)] = n
            if tb < NTB - 1:
                for c in range(4):
                    n += 16
                    trF[(tb, c)] = n
        n += 16
        trF[(NTB - 1, 3)] = n
        for st in range(3):
            n += 16
            stS[st] = n
        ocF = {}
        n2 = s_cast.n
        for tb in range(NTB - 1):
            for c in range(4):
                n2 += 16
                ocF[(tb, c)] = n2
        n2 += 16
        ocF[(NTB - 1, 3)] = n2
        trb = {}

        if True:
            @blk.sync
            def _(s):
                for tb in range(NTB - 1):
                    for m in range(8):
                        s.wait_ge(s_ve.sem, nmF[(tb, m)])
                        s_dma.dma2(s, lambda: s.dma_start(
                            out=outb[m * P:(m + 1) * P, tb * TB:(tb + 1) * TB],
                            in_=ofb[:, m % 4, :]))
                    for mm_ in range(8):
                        s_dma.wait(s, obF[(tb, mm_)])
                    for c in range(4):
                        if c >= 2:
                            s_cast.wait(s, ocF[(tb, c - 2)])
                        elif tb >= 1:
                            s_cast.wait(s, ocF[(tb - 1, c + 2)])
                        s_dma.dma2(s, lambda: s.dma_start_transpose(
                            otm[:, c % 2, :], outb[:, (tb * 4 + c) * P:(tb * 4 + c + 1) * P]))
                tb = NTB - 1
                for m in range(8):
                    s.wait_ge(s_ve.sem, nmF[(tb, m)])
                    s_dma.dma2(s, lambda: s.dma_start(
                        out=outb[m * P:(m + 1) * P, tb * TB + 3 * P:(tb + 1) * TB],
                        in_=ofb[:, m % 4, 3 * P:TB]))
                for mm_ in range(8):
                    s_dma.wait(s, obF[(tb, mm_)])
                s.wait_ge(s_mm.sem, stF[(tb, 7)])
                s_dma.dma2(s, lambda: s.dma_start_transpose(
                    ot3[:, 0, :], outb[:, (tb * 4 + 3) * P:(tb * 4 + 4) * P]))
                for st in range(3):
                    s.wait_ge(s_sc.sem, ev3[st])
                    s_dma.dma2(s, lambda: s.dma_start(
                        out=out[(tb * 4 + st) * P:(tb * 4 + st + 1) * P, :],
                        in_=otf[:, 0, :]))
                    s_dma.wait(s, stS[st])

            @blk.gpsimd
            def _(g):
                for tb in range(NTB - 1):
                    for c in range(4):
                        s_dma.wait(g, trF[(tb, c)])
                        s_cast.dma2(g, lambda: g.dma_start(
                            out=out[(tb * 4 + c) * P:(tb * 4 + c + 1) * P, :],
                            in_=otm[:, c % 2, :]))
                s_dma.wait(g, trF[(NTB - 1, 3)])
                s_cast.dma2(g, lambda: g.dma_start(
                    out=out[(NTB * 4 - 1) * P:NTB * 4 * P, :], in_=ot3[:, 0, :]))
                s_cast.wait_all(g)

            @blk.tensor
            def _(t):
                for op in t_ops:
                    kind, tb, m = op
                    gm = tb * 8 + m
                    if kind == "G":
                        par = gm % 2
                        hh, mh = divmod(m, 4)
                        if gm == 0:
                            t.wait_ge(s_sc.sem, seamD["siluD"][23])
                        if gm >= 2:
                            t.wait_ge(s_ve.sem, evF[divmod(gm - 2, 8)])
                        ins = None
                        for k in range(HC):
                            if gm == 0:
                                t.wait_ge(s_ve.sem, seamD["hdone"][k])
                                s_dma.wait(t, seamD["wd_str"][0][k])
                            elif gm < 8 and mh == 0:
                                s_dma.wait(t, seamD["wd_str"][hh][k])
                            ins = t.matmul(
                                pmm[par],
                                wdh[hh][:, k, mh * P:(mh + 1) * P],
                                h_fm[:, k, tb * TB:(tb + 1) * TB],
                                start=(k == 0), stop=(k == HC - 1),
                            )
                        ins.then_inc(s_mm.sem, 1)
                    elif kind == "S":
                        t.wait_ge(s_ve.sem, evF[(tb, m)])
                        t.wait_ge(s_sc.sem, sqF[(tb, m)])
                        if m == 0 and tb >= 1:
                            t.wait_ge(s_ve.sem, vvF[tb - 1])
                        t.matmul(pS, onesb2[:, 0:1], r2f[:, slot(tb, m), :],
                                 start=(m == 0), stop=(m == 7))
                        ins = t.matmul(pQ, onesb2[:, 0:1], sq2[:, m % 2, :],
                                       start=(m == 0), stop=(m == 7))
                        ins.then_inc(s_mm.sem, 1)
                    else:
                        t.wait_ge(s_sc.sem, leF[tb])
                        if tb >= 1:
                            t.wait_ge(s_ve.sem, nmF[(tb - 1, 7)])
                        t.matmul(pMB, onesr2[:, :], meant2[:, :], start=True, stop=True)
                        ins = t.matmul(pRB, onesr2[:, :], rstd2[:, :], start=True, stop=True)
                        ins.then_inc(s_mm.sem, 1)
                tb3 = NTB - 1
                nmm = max(list(bcF.values()) + list(stF.values())
                          + list(mmF.values()))
                for m in range(8):
                    trb[m] = nmm + m + 1
                    t.wait_ge(s_ve.sem, nmF[(tb3, m)])
                    ins = None
                    for st in range(3):
                        ins = t.matmul(
                            strip[st][:, m * P:(m + 1) * P],
                            ofb[:, m % 4, st * P:(st + 1) * P], ident[:],
                            start=True, stop=True)
                    ins.then_inc(s_mm.sem, 1)

            @blk.vector
            def _(v):
                v.memset(onesb2[:], 1.0)
                ins = v.memset(onesr2[:], 1.0)
                ins.then_inc(s_ve.sem, 1)
                for op in v_ops:
                    kind, tb, x = op
                    if kind == "EV":
                        gm = tb * 8 + x
                        v.wait_ge(s_mm.sem, mmF[(tb, x)])
                        ins = v.scalar_tensor_tensor(
                            r2f[:, slot(tb, x), :], pmm[gm % 2], SCALE,
                            pl_fm[:, x, tb * TB:(tb + 1) * TB],
                            ALU.mult, ALU.add)
                        ins.then_inc(s_ve.sem, 1)
                    elif kind == "MC":
                        v.wait_ge(s_mm.sem, stF[(tb, 7)])
                        v.tensor_scalar(meant2[:], pS, 1.0 / D, None, ALU.mult)
                        v.tensor_mul(vart2[:], meant2[:], meant2[:])
                        ins = v.scalar_tensor_tensor(vart2[:], pQ, 1.0 / D,
                                                     vart2[:], ALU.mult, ALU.subtract)
                        ins.then_inc(s_ve.sem, 1)
                    elif kind == "NM":
                        m = x
                        if m == 0:
                            v.wait_ge(s_mm.sem, bcF[tb])
                        if (tb, m - 4) in obF:
                            s_dma.wait(v, obF[(tb, m - 4)])
                            if tb == NTB - 1:
                                v.wait_ge(s_mm.sem, trb[m - 4])
                        elif tb >= 1 and m < 4:
                            s_dma.wait(v, obF[(tb - 1, m + 4)])
                        v.tensor_sub(ofb[:, m % 4, :], r2f[:, slot(tb, m), :], pMB)
                        ins = v.tensor_mul(ofb[:, m % 4, :], ofb[:, m % 4, :], pRB)
                        ins.then_inc(s_ve.sem, 1)


            @blk.scalar
            def _(sc):
                for op in c_ops:
                    kind, tb, m = op
                    if kind == "SQ":
                        sc.wait_ge(s_ve.sem, evF[(tb, m)])
                        if (tb, m - 2) in stF and (tb * 8 + m) >= 2:
                            sc.wait_ge(s_mm.sem, stF[(tb, m - 2)])
                        elif tb >= 1 and m < 2:
                            sc.wait_ge(s_mm.sem, stF[(tb - 1, m + 6)])
                        ins = sc.activation(sq2[:, m % 2, :],
                                            r2f[:, slot(tb, m), :], AF.Square)
                        ins.then_inc(s_sc.sem, 1)
                    else:
                        sc.wait_ge(s_ve.sem, vvF[tb])
                        sc.activation(vart2[:], vart2[:], AF.Ln, bias=EPS)
                        ins = sc.activation(rstd2[:], vart2[:], AF.Exp, scale=-0.5)
                        ins.then_inc(s_sc.sem, 1)
                for st in range(3):
                    if st == 0:
                        sc.wait_ge(s_mm.sem, trb[7])
                        s_cast.wait(sc, ocF[(NTB - 2, 3)])
                    else:
                        s_dma.wait(sc, stS[st - 1])
                    ins = sc.activation(otf[:, 0, :], strip[st][:, 0:D],
                                        AF.Copy)
                    ins.then_inc(s_sc.sem, 1)

        s_mm.n = trb[7]
        s_ve.n = nmF[(NTB - 1, 7)]
        s_sc.n = leF[NTB - 1] + 3
        s_dma.n = stS[2]
        s_cast.n = ocF[(NTB - 1, 3)]


_NC_CACHE = None


def kernel(L0_attn_x, L0_attn_W_in, L0_attn_W_out, L0_attn_conv_w, L0_attn_conv_b,
           L0_attn_dt_w, L0_attn_dt_b, L0_attn_A, L0_attn_Dp,
           L0_ffn_w_gate, L0_ffn_w_up, L0_ffn_w_down):
    global _NC_CACHE
    x = np.ascontiguousarray(np.asarray(L0_attn_x, dtype=np.float32))
    com = dict(
        w_in=np.ascontiguousarray(np.asarray(L0_attn_W_in, dtype=np.float32)),
        w_out=np.ascontiguousarray(np.asarray(L0_attn_W_out, dtype=np.float32)),
        conv_w=np.ascontiguousarray(np.asarray(L0_attn_conv_w, dtype=np.float32)),
        conv_b=np.ascontiguousarray(np.asarray(L0_attn_conv_b, dtype=np.float32)),
        dt_w=np.ascontiguousarray(np.asarray(L0_attn_dt_w, dtype=np.float32)),
        dt_b=np.ascontiguousarray(np.asarray(L0_attn_dt_b, dtype=np.float32)),
        A=np.ascontiguousarray(np.asarray(L0_attn_A, dtype=np.float32)),
        Dp=np.ascontiguousarray(np.asarray(L0_attn_Dp, dtype=np.float32)),
        wg=np.ascontiguousarray(np.asarray(L0_ffn_w_gate, dtype=np.int32)),
        wu=np.ascontiguousarray(np.asarray(L0_ffn_w_up, dtype=np.int32)),
        wd=np.ascontiguousarray(np.asarray(L0_ffn_w_down, dtype=np.int32)),
    )
    if _NC_CACHE is None:
        _NC_CACHE = build()
    in_maps = [dict(x=np.ascontiguousarray(x[b]), **com) for b in range(8)]
    res = run_bass_kernel_spmd(_NC_CACHE, in_maps, core_ids=list(range(8)))
    return np.stack([np.asarray(r["out"]) for r in res.results], axis=0)


# version 56
# speedup vs baseline: 1.0863x; 1.0164x over previous
import contextlib
import numpy as np
import concourse.bass as bass
import concourse.mybir as mybir
from concourse.bass_utils import run_bass_kernel_spmd

F32 = mybir.dt.float32
BF16 = mybir.dt.bfloat16
I32 = mybir.dt.int32
AF = mybir.ActivationFunctionType
ALU = mybir.AluOpType

D, DC = 1024, 8
CI, CC = 1536, 12
H, HC = 3072, 24
L, TC = 2048, 16
NT, TTW = 4, 512
SCALE = 1.0 / 127.0
EPS = 1e-5
P = 128


class S:

    def __init__(self, nc, name):
        self.sem = nc.alloc_semaphore(name)
        self.n = 0

    def dma(self, inst):
        inst.then_inc(self.sem, 16)
        self.n += 16
        return self.n

    def inc(self, inst):
        inst.then_inc(self.sem, 1)
        self.n += 1
        return self.n


class RingS:

    def __init__(self, nc, name, depth):
        self.sems = [nc.alloc_semaphore(f"{name}{j}") for j in range(depth)]
        self.depth = depth
        self.n = 0

    def dma2(self, eng, fn):
        i = self.n // 16
        if i >= self.depth:
            eng.wait_ge(self.sems[i % self.depth], 16 * (i // self.depth))
        ins = fn()
        ins.then_inc(self.sems[i % self.depth], 16)
        self.n += 16
        return self.n

    def wait(self, eng, val):
        if val <= 0:
            return
        i = val // 16 - 1
        eng.wait_ge(self.sems[i % self.depth], 16 * (i // self.depth + 1))

    def wait_all(self, eng):
        total = self.n // 16
        for j in range(min(self.depth, total)):
            i = total - 1 - j
            eng.wait_ge(self.sems[i % self.depth], 16 * (i // self.depth + 1))


def build():
    nc = bass.Bass("TRN2", target_bir_lowering=False)

    x_in = nc.dram_tensor("x", [L, D], F32, kind="ExternalInput").ap()
    w_in = nc.dram_tensor("w_in", [H, D], F32, kind="ExternalInput").ap()
    w_out = nc.dram_tensor("w_out", [D, CI], F32, kind="ExternalInput").ap()
    conv_w = nc.dram_tensor("conv_w", [CI, 1, 3], F32, kind="ExternalInput").ap()
    conv_b = nc.dram_tensor("conv_b", [CI], F32, kind="ExternalInput").ap()
    dt_w = nc.dram_tensor("dt_w", [CI, CI], F32, kind="ExternalInput").ap()
    dt_b = nc.dram_tensor("dt_b", [CI], F32, kind="ExternalInput").ap()
    A_in = nc.dram_tensor("A", [CI], F32, kind="ExternalInput").ap()
    Dp_in = nc.dram_tensor("Dp", [CI], F32, kind="ExternalInput").ap()
    wg = nc.dram_tensor("wg", [H, D], I32, kind="ExternalInput").ap()
    wu = nc.dram_tensor("wu", [H, D], I32, kind="ExternalInput").ap()
    wd = nc.dram_tensor("wd", [D, H], I32, kind="ExternalInput").ap()
    out = nc.dram_tensor("out", [L, D], F32, kind="ExternalOutput").ap()

    xbf = nc.dram_tensor("xbf", [L, D], BF16).ap()
    wib = nc.dram_tensor("wib", [H, D], BF16).ap()
    dtwb = nc.dram_tensor("dtwb", [CI, CI], BF16).ap()
    wob = nc.dram_tensor("wob", [D, CI], BF16).ap()
    wgb = nc.dram_tensor("wgb", [H, D], BF16).ap()
    wub = nc.dram_tensor("wub", [H, D], BF16).ap()
    wdb = nc.dram_tensor("wdb", [D, H], BF16).ap()
    attnb = nc.dram_tensor("attnb", [D, L], BF16).ap()
    postb = nc.dram_tensor("postb", [D, L], BF16).ap()
    ffnb = nc.dram_tensor("ffnb", [D, L], BF16).ap()

    s_cast = RingS(nc, "s_cast", 4)
    s_dma = RingS(nc, "s_dma", 8)
    s_mm = S(nc, "s_mm")
    s_sc = S(nc, "s_sc")
    s_ve = S(nc, "s_ve")

    top = contextlib.ExitStack()
    with top:
        sb = lambda name, shape, dt: top.enter_context(nc.sbuf_tensor(name, shape, dt))
        cwk = sb("cwk", [P, 3, CC], F32)
        cbv = sb("cbv", [P, CC], F32)
        dbv = sb("dbv", [P, CC], F32)
        Av = sb("Av", [P, CC], F32)
        Dpv = sb("Dpv", [P, CC], F32)
        ones = sb("ones", [P, 1], F32)
        stats = sb("stats", [P, 2, 3], F32)
        cst = sb("cst", [P, 5], F32)
        ident = sb("ident", [P, P], BF16)
        zt = sb("zt", [P, P + 1], BF16)
        idd = nc.dram_tensor("idd", [P, P + 1], BF16).ap()
        cvals = [EPS, SCALE, -0.5, 1.0 / D, -1.0 / D]
        for i, vv in enumerate(cvals):
            nc.const_aps.register(F32, vv, cst[:, i:i + 1])
        psA = top.enter_context(nc.psum_tensor("psA", [P, L], F32))
        psB = top.enter_context(nc.psum_tensor("psB", [P, L], F32))
        ps = [psA, psB]

        _prologue(nc, s_cast, s_dma, x_in, w_in, dt_w, w_out, wg, wu, wd,
                  xbf, wib, dtwb, wob, wgb, wub, wdb,
                  conv_w, conv_b, dt_b, A_in, Dp_in, cwk, cbv, dbv, Av, Dpv, ones,
                  cst, cvals, ident, zt, idd)
        m_cast = dict(wibL=16, wibR=32, wib2=48, dtwb=64, wob=80, wgb=96,
                      wub=112, wdb=128)

        pl_stack = contextlib.ExitStack()
        mid = contextlib.ExitStack()
        blkstack = contextlib.ExitStack()
        blk = blkstack.enter_context(nc.Block())
        x_fm = pl_stack.enter_context(nc.sbuf_tensor("x_fm", [P, DC, L], BF16))
        with mid:
            yz = mid.enter_context(nc.sbuf_tensor("yz", [P, CC, L], BF16))
            ab = contextlib.ExitStack()
            with ab:
                x_conv = ab.enter_context(nc.sbuf_tensor("x_conv", [P, CC, L], BF16))
                casts = [(wib, w_in), (dtwb, dt_w), (wob, w_out),
                         (wgb, wg), (wub, wu), (wdb, wd)]
                mmA, scA = _phase_a(nc, blk, s_cast, s_dma, s_mm, s_sc, s_ve,
                                    m_cast, x_in, wib, x_conv, yz, cwk, cbv,
                                    ps, casts, x_fm, ident)
                m_cast["_m_dtwT"] = s_dma.n + 12 * 16
                wo_stack = contextlib.ExitStack()
                woT = wo_stack.enter_context(
                    nc.sbuf_tensor("woT", [P, CC, D], BF16, side="right"))
                m_woT, seamB = _phase_b(nc, blk, s_cast, s_dma, s_mm, s_sc,
                                        s_ve, m_cast, dtwb, x_conv, yz, dbv,
                                        Av, Dpv, ones, ps,
                                        [casts[4], casts[5]], woT, wob,
                                        mmA, scA)
            plnorm, seamC = _phase_c(nc, blk, s_cast, s_dma, s_mm, s_sc, s_ve,
                                     m_cast, wob, x_fm, yz, woT, m_woT,
                                     ones, cst, ps, seamB)
            wo_stack.close()

        with pl_stack:
            dstk = contextlib.ExitStack()
            with dstk:
                h_fm = dstk.enter_context(nc.sbuf_tensor("h_fm", [P, HC, L], BF16))
                wdh = [dstk.enter_context(nc.sbuf_tensor(f"wdh{i}", [P, HC, L // 4], BF16)) for i in range(2)]
                seamD = _phase_d1(nc, blk, s_cast, s_dma, s_mm, s_sc, s_ve, m_cast,
                                  plnorm, wgb, wub, x_fm, h_fm, psA, psB,
                                  wdb, wdh, seamB, seamC)
                _phase_d2e(nc, blk, s_cast, s_dma, s_mm, s_sc, s_ve, m_cast,
                           wdb, ffnb, h_fm, x_fm, out, ps, wdh, seamD, ident)
            blkstack.close()

    return nc


def _prologue(nc, s_cast, s_dma, x_in, w_in, dt_w, w_out, wg, wu, wd,
              xbf, wib, dtwb, wob, wgb, wub, wdb,
              conv_w, conv_b, dt_b, A_in, Dp_in, cwk, cbv, dbv, Av, Dpv, ones,
              cst, cvals, ident, zt, idd):
    pv = nc.alloc_semaphore("pv")
    with nc.Block() as blk:
        @blk.sync
        def _(s):
            kw = dict(allow_slow_non_contiguous=True)
            s_dma.dma2(s, lambda: s.dma_start(out=cwk[:], in_=conv_w.rearrange("(j p) o k -> p (o k) j", p=P), **kw))
            s_dma.dma2(s, lambda: s.dma_start(out=cbv[:], in_=conv_b.rearrange("(j p) -> p j", p=P), **kw))
            s_dma.dma2(s, lambda: s.dma_start(out=dbv[:], in_=dt_b.rearrange("(j p) -> p j", p=P), **kw))
            s_dma.dma2(s, lambda: s.dma_start(out=Av[:], in_=A_in.rearrange("(j p) -> p j", p=P), **kw))
            s_dma.dma2(s, lambda: s.dma_start(out=Dpv[:], in_=Dp_in.rearrange("(j p) -> p j", p=P), **kw))
            s.wait_ge(pv, 1)
            m1 = s_dma.dma2(s, lambda: s.dma_start(out=idd, in_=zt[:]))
            s_dma.wait(s, m1)
            iv = idd.rearrange("p q -> (p q)").rearrange("(a b) -> a b", b=P)
            s_dma.dma2(s, lambda: s.dma_start(out=ident[:], in_=iv[0:P, :]))

        @blk.vector
        def _(v):
            v.memset(ones[:], 1.0)
            for i, vv in enumerate(cvals):
                v.memset(cst[:, i:i + 1], vv)
            v.memset(zt[:, 1:P + 1], 0.0)
            ins = v.memset(zt[:, 0:1], 1.0)
            ins.then_inc(pv, 1)


def _phase_a(nc, blk, s_cast, s_dma, s_mm, s_sc, s_ve, m_cast, x_in, wib,
             x_conv, yz, cwk, cbv, ps, casts, x_fm, ident):
    stk = contextlib.ExitStack()
    with stk:
        wiT = stk.enter_context(nc.sbuf_tensor("wiT", [P, DC, CI], BF16))
        wiT2 = stk.enter_context(nc.sbuf_tensor("wiT2", [P, DC, CI], BF16))
        stg1a = stk.enter_context(nc.sbuf_tensor("stg1a", [P, L], F32))
        stg1b = stk.enter_context(nc.sbuf_tensor("stg1b", [P, L], F32))
        stg2 = stk.enter_context(nc.sbuf_tensor("stg2", [P, L], F32))
        stg1 = [stg1a, stg1b]
        xtm = nc.alloc_sbuf_tensor_at("xtm", (P, TC, D), BF16,
                                      offset=nc.lookup_mloc(yz).addr)
        xf32 = nc.alloc_sbuf_tensor_at("xf32", (P, 4, D), F32,
                                       offset=nc.lookup_mloc(stg1a).addr)

        xld = [s_dma.n + 16 * (c + 1) for c in range(TC)]
        trmm = [s_mm.n + c + 1 for c in range(TC)]
        cast_ms, evac_ms = [0] * TC, [0] * TC
        n = s_sc.n
        sc_plan = []
        for c in range(TC):
            sc_plan.append(("C", c))
            if c >= 3:
                sc_plan.append(("E", c - 3))
        sc_plan += [("E", c) for c in range(TC - 3, TC)]
        for op, c in sc_plan:
            n += 1
            (cast_ms if op == "C" else evac_ms)[c] = n
        mmA = [trmm[-1] + m + 1 for m in range(24)]
        veA = [s_ve.n + m + 1 for m in range(12)]
        scA = [n + m + 1 for m in range(24)]
        trA = [xld[-1] + 16 * (k + 1) for k in range(DC)]
        m_wiT1 = trA[-1]
        m_wiT2 = m_wiT1 + 8 * 16
        m_cast["_m_dtwT_plan"] = m_wiT2 + 12 * 16

        if True:
            @blk.gpsimd
            def _(g):
                (wib_d, wsrc) = casts[0]
                s_cast.dma2(g, lambda: g.dma_start(out=wib_d[0:CI, 0:D // 2], in_=wsrc[0:CI, 0:D // 2]))
                s_cast.dma2(g, lambda: g.dma_start(out=wib_d[0:CI, D // 2:D], in_=wsrc[0:CI, D // 2:D]))
                s_dma.wait(g, m_wiT1)
                s_cast.dma2(g, lambda: g.dma_start(out=wib_d[CI:H, :], in_=wsrc[CI:H, :]))
                s_dma.wait(g, m_wiT2)
                s_cast.dma2(g, lambda: g.dma_start(out=casts[1][0], in_=casts[1][1]))
                s_cast.dma2(g, lambda: g.dma_start(out=casts[2][0], in_=casts[2][1]))
                s_cast.dma2(g, lambda: g.dma_start(out=casts[3][0], in_=casts[3][1]))

            @blk.sync
            def _(s):
                for c in range(TC):
                    if c >= 4:
                        s.wait_ge(s_sc.sem, cast_ms[c - 4])
                    s_dma.dma2(s, lambda: s.dma_start(
                        out=xf32[:, c % 4, :], in_=x_in[c * P:(c + 1) * P, :]))
                for k in range(DC):
                    s_cast.wait(s, m_cast["wibL" if k < DC // 2 else "wibR"])
                    s_dma.dma2(s, lambda: s.dma_start_transpose(wiT[:, k, :], wib[0:CI, k * P:(k + 1) * P]))
                s_cast.wait(s, m_cast["wib2"])
                for k in range(DC):
                    s_dma.dma2(s, lambda: s.dma_start_transpose(wiT2[:, k, :], wib[CI:H, k * P:(k + 1) * P]))

            @blk.tensor
            def _(t):
                for c in range(TC):
                    t.wait_ge(s_sc.sem, cast_ms[c])
                    if c >= 2:
                        t.wait_ge(s_sc.sem, evac_ms[c - 2])
                    base = (c % 2) * (L // 2)
                    ins = None
                    for k in range(DC):
                        ins = t.matmul(
                            ps[0][:, base + k * P:base + (k + 1) * P],
                            xtm[:, c, k * P:(k + 1) * P], ident[:],
                            start=True, stop=True)
                    ins.then_inc(s_mm.sem, 1)
                for m in range(24):
                    mm = m % 12
                    pp = m % 2
                    wt = wiT if m < 12 else wiT2
                    if m == 0:
                        t.wait_ge(s_sc.sem, evac_ms[-1])
                    if 2 <= m < 14:
                        t.wait_ge(s_ve.sem, veA[m - 2])
                    elif m >= 14:
                        t.wait_ge(s_sc.sem, scA[m - 2])
                    ins = None
                    for k in range(DC):
                        if m == 0:
                            s_dma.wait(t, trA[k])
                        elif m == 12:
                            s_dma.wait(t, m_wiT1 + (k + 1) * 16)
                        for tt in range(NT):
                            ins = t.matmul(
                                ps[pp][:, tt * TTW:(tt + 1) * TTW],
                                wt[:, k, mm * P:(mm + 1) * P],
                                x_fm[:, k, tt * TTW:(tt + 1) * TTW],
                                start=(k == 0), stop=(k == DC - 1),
                            )
                    ins.then_inc(s_mm.sem, 1)

            @blk.vector
            def _(v):
                for m in range(12):
                    pp = m % 2
                    if m >= 2:
                        v.wait_ge(s_sc.sem, scA[m - 2])
                    v.wait_ge(s_mm.sem, mmA[m])
                    v.tensor_scalar(stg1[pp][:, 1:L], ps[pp][:, 0:L - 1],
                                    cwk[:, 0, m:m + 1], None, ALU.mult)
                    v.tensor_scalar(stg2[:, 0:1], ps[pp][:, 0:1],
                                    cwk[:, 1, m:m + 1], None, ALU.mult)
                    v.scalar_tensor_tensor(stg2[:, 1:L], ps[pp][:, 1:L],
                                           cwk[:, 1, m:m + 1],
                                           stg1[pp][:, 1:L], ALU.mult, ALU.add)
                    v.scalar_tensor_tensor(stg1[pp][:, 0:L - 1], ps[pp][:, 1:L],
                                           cwk[:, 2, m:m + 1], stg2[:, 0:L - 1],
                                           ALU.mult, ALU.add)
                    ins = v.tensor_copy(stg1[pp][:, L - 1:L], stg2[:, L - 1:L])
                    ins.then_inc(s_ve.sem, 1)

            @blk.scalar
            def _(sc):
                for op, c in sc_plan:
                    if op == "C":
                        s_dma.wait(sc, xld[c])
                        ins = sc.activation(xtm[:, c, :], xf32[:, c % 4, :],
                                            AF.Copy)
                    else:
                        sc.wait_ge(s_mm.sem, trmm[c])
                        base = (c % 2) * (L // 2)
                        ins = sc.activation(x_fm[:, :, c * P:(c + 1) * P],
                                            ps[0][:, base:base + D], AF.Copy)
                    ins.then_inc(s_sc.sem, 1)
                for m in range(12):
                    pp = m % 2
                    sc.wait_ge(s_ve.sem, veA[m])
                    ins = sc.activation(x_conv[:, m, :], stg1[pp][:], AF.Silu,
                                        bias=cbv[:, m:m + 1])
                    ins.then_inc(s_sc.sem, 1)
                for m in range(12, 24):
                    pp = m % 2
                    sc.wait_ge(s_mm.sem, mmA[m])
                    ins = sc.activation(yz[:, m - 12, :], ps[pp][:, :], AF.Silu)
                    ins.then_inc(s_sc.sem, 1)

        s_mm.n = mmA[-1]
        s_ve.n = veA[-1]
        s_sc.n = scA[-1]
        s_dma.n = m_wiT2
        return mmA, scA


def _phase_b(nc, blk, s_cast, s_dma, s_mm, s_sc, s_ve, m_cast, dtwb,
             x_conv, yz, dbv, Av, Dpv, ones, ps, late_casts, woT, wob,
             mmA, scA):
    psA, psB = ps
    HF = L // 2
    stk = contextlib.ExitStack()
    with stk:
        dtwT = stk.enter_context(nc.sbuf_tensor("dtwT", [P, CC, CI], BF16))
        bsA = [stk.enter_context(nc.sbuf_tensor(f"bsA{i}", [P, HF], BF16)) for i in range(2)]
        bsB = [stk.enter_context(nc.sbuf_tensor(f"bsB{i}", [P, HF], BF16)) for i in range(2)]
        bsC = [stk.enter_context(nc.sbuf_tensor(f"bsC{i}", [P, HF], BF16)) for i in range(2)]
        carryv = stk.enter_context(nc.sbuf_tensor("carryv", [P, CC], F32))

        pr = [psA[:, 0:HF], psA[:, HF:L], psB[:, 0:HF], psB[:, HF:L]]

        mmB = [s_mm.n + g + 1 for g in range(24)]
        pfree = [s_sc.n + 2 * g + 1 for g in range(24)]
        ready = [s_sc.n + 2 * g + 2 for g in range(24)]
        veB = [s_ve.n + g + 1 for g in range(24)]
        m_dtwT = [s_dma.n + (k + 1) * 16 for k in range(CC)]
        m_woT = [m_dtwT[-1] + (k + 1) * 16 for k in range(CC)]
        ones_hb = ones[:, 0:1].broadcast_to((P, HF))

        if True:
            @blk.gpsimd
            def _(g):
                s_dma.wait(g, m_woT[-1])
                for dst, srcp in late_casts:
                    s_cast.dma2(g, lambda: g.dma_start(out=dst, in_=srcp))

            @blk.sync
            def _(s):
                s_cast.wait(s, m_cast["dtwb"])
                s.wait_ge(s_mm.sem, mmA[23])
                for k in range(CC):
                    s_dma.dma2(s, lambda: s.dma_start_transpose(
                        dtwT[:, k, :], dtwb[0:CI, k * P:(k + 1) * P]))
                s_cast.wait(s, m_cast["wob"])
                for k in range(CC):
                    s_dma.dma2(s, lambda: s.dma_start_transpose(
                        woT[:, k, :], wob[0:D, k * P:(k + 1) * P]))

            bpar = lambda g: (g % 4) if g < 20 else 2 + (g % 2)

            @blk.tensor
            def _(t):
                for g in range(24):
                    h, m = divmod(g, 12)
                    par = bpar(g)
                    if g < 2:
                        t.wait_ge(s_sc.sem, scA[22])
                    elif g < 4:
                        t.wait_ge(s_sc.sem, scA[23])
                    elif g < 20:
                        t.wait_ge(s_sc.sem, pfree[g - 4])
                    else:
                        t.wait_ge(s_sc.sem, pfree[g - 2])
                    ins = None
                    for k in range(CC):
                        if g == 0:
                            t.wait_ge(s_sc.sem, scA[k])
                        if h == 0 and m == 0:
                            s_dma.wait(t, m_dtwT[k])
                        for tt in range(2):
                            ins = t.matmul(
                                pr[par][:, tt * TTW:(tt + 1) * TTW],
                                dtwT[:, k, m * P:(m + 1) * P],
                                x_conv[:, k, h * HF + tt * TTW:h * HF + (tt + 1) * TTW],
                                start=(k == 0), stop=(k == CC - 1),
                            )
                    ins.then_inc(s_mm.sem, 1)

            @blk.scalar
            def _(sc):
                for g in range(24):
                    h, m = divmod(g, 12)
                    par = bpar(g)
                    ss = g % 2
                    sc.wait_ge(s_mm.sem, mmB[g])
                    if g >= 2:
                        sc.wait_ge(s_ve.sem, veB[g - 2])
                    ins = sc.activation(bsA[ss][:], pr[par], AF.Exp,
                                        bias=dbv[:, m:m + 1])
                    ins.then_inc(s_sc.sem, 1)
                    sc.activation(bsB[ss][:], bsA[ss][:], AF.Ln, bias=1.0)
                    ins = sc.activation(bsC[ss][:], bsB[ss][:], AF.Exp,
                                        scale=Av[:, m:m + 1])
                    ins.then_inc(s_sc.sem, 1)

            @blk.vector
            def _(v):
                for g in range(24):
                    h, m = divmod(g, 12)
                    ss = g % 2
                    xs = x_conv[:, m, h * HF:(h + 1) * HF]
                    v.wait_ge(s_sc.sem, ready[g])
                    v.tensor_mul(bsA[ss][:], bsB[ss][:], xs)
                    v.tensor_mul(bsA[ss][:], bsA[ss][:], bsC[ss][:])
                    v.tensor_tensor_scan(bsB[ss][:], ones_hb, bsA[ss][:], 0.0,
                                         ALU.mult, ALU.add)
                    if h == 0:
                        v.tensor_scalar(carryv[:, m:m + 1],
                                        bsB[ss][:, HF - 1:HF], ones[:, 0:1],
                                        None, ALU.mult)
                    else:
                        v.tensor_scalar(bsB[ss][:], bsB[ss][:], carryv[:, m:m + 1],
                                        None, ALU.add)
                    v.tensor_mul(bsB[ss][:], bsB[ss][:], bsC[ss][:])
                    v.tensor_scalar(bsA[ss][:], xs, Dpv[:, m:m + 1], None, ALU.mult)
                    v.tensor_add(bsB[ss][:], bsB[ss][:], bsA[ss][:])
                    ins = v.tensor_mul(yz[:, m, h * HF:(h + 1) * HF], bsB[ss][:],
                                       yz[:, m, h * HF:(h + 1) * HF])
                    ins.then_inc(s_ve.sem, 1)

        s_mm.n = mmB[-1]
        s_ve.n = veB[-1]
        s_sc.n = ready[-1]
        s_dma.n = m_woT[-1]
        return m_woT, dict(mmB=mmB, veB=veB, pfree=pfree, ready=ready)


def _phase_c(nc, blk, s_cast, s_dma, s_mm, s_sc, s_ve, m_cast,
             wob, x_fm, yz, woT, m_woT, ones, cst, ps, seamB):
    psA, psB = ps
    HF = L // 2
    stk = contextlib.ExitStack()
    with stk:
        sq1 = stk.enter_context(nc.sbuf_tensor("sq1", [P, 2, HF], BF16))
        onesb = stk.enter_context(nc.sbuf_tensor("onesb", [P, 1], BF16))
        onesr = stk.enter_context(nc.sbuf_tensor("onesr", [1, P], BF16))
        vart = stk.enter_context(nc.sbuf_tensor("vart", [1, L], F32))
        meant = stk.enter_context(nc.sbuf_tensor("meant", [1, L], BF16))
        rstdt = stk.enter_context(nc.sbuf_tensor("rstdt", [1, L], BF16))
        mbc = stk.enter_context(nc.sbuf_tensor("mbc", [P, L], BF16))
        rbc = stk.enter_context(nc.sbuf_tensor("rbc", [P, L], BF16))

        pmm = [psA[:, 0:HF], psA[:, HF:L]]
        pS = psB[0:1, 0:HF]
        pQ = psB[0:1, HF:L]

        t_ops = []
        pend = []
        NG = 16
        for g in range(NG + 6):
            if g < NG:
                t_ops.append(("G",) + divmod(g, 8))
            for it in list(pend):
                if it[3] <= g:
                    t_ops.append(it[:3])
                    pend.remove(it)
            if g < NG:
                h, m = divmod(g, 8)
                pend.append(("S", h, m, g + 2))
                if g == NG - 1:
                    pend.append(("BC", 0, 0, g + 4))
        mmC, stC = {}, {}
        bcastC = [0]
        n = s_mm.n
        for op in t_ops:
            n += 1
            if op[0] == "G":
                mmC[(op[1], op[2])] = n
            elif op[0] == "S":
                stC[(op[1], op[2])] = n
            else:
                bcastC[0] = n
        bcastC = bcastC[0]

        v0 = s_ve.n + 1
        residC, mcC = {}, {}
        n = v0
        for g in range(NG):
            h, m = divmod(g, 8)
            n += 1
            residC[(h, m)] = n
            if m == 7:
                n += 1
                mcC[h] = n
        bcevC = n + 1
        normC = [bcevC + m + 1 for m in range(8)]
        sqC, leC = {}, {}
        n = s_sc.n
        for g in range(NG):
            h, m = divmod(g, 8)
            n += 1
            sqC[(h, m)] = n
            if m == 7:
                n += 1
                leC[h] = n
        if True:
            @blk.tensor
            def _(t):
                for op in t_ops:
                    kind, h, m = op
                    if kind == "G":
                        g = h * 8 + m
                        par = g % 2
                        if g < 2:
                            t.wait_ge(s_sc.sem, seamB["pfree"][16 + g])
                        else:
                            prev = divmod(g - 2, 8)
                            t.wait_ge(s_ve.sem, residC[prev])
                        ins = None
                        for k in range(CC):
                            if g == 0:
                                s_dma.wait(t, m_woT[k])
                            if m == 0:
                                t.wait_ge(s_ve.sem, seamB["veB"][h * 12 + k])
                            for tt in range(2):
                                ins = t.matmul(
                                    pmm[par][:, tt * TTW:(tt + 1) * TTW],
                                    woT[:, k, m * P:(m + 1) * P],
                                    yz[:, k, h * HF + tt * TTW:h * HF + (tt + 1) * TTW],
                                    start=(k == 0), stop=(k == CC - 1),
                                )
                        ins.then_inc(s_mm.sem, 1)
                    elif kind == "S":
                        g = h * 8 + m
                        t.wait_ge(s_ve.sem, residC[(h, m)])
                        t.wait_ge(s_sc.sem, sqC[(h, m)])
                        if h == 0 and m == 0:
                            t.wait_ge(s_sc.sem, seamB["pfree"][23])
                        if h == 1 and m == 0:
                            t.wait_ge(s_ve.sem, mcC[0])
                        ins = None
                        for tt in range(2):
                            t.matmul(pS[:, tt * TTW:(tt + 1) * TTW], onesb[:, 0:1],
                                     x_fm[:, m, h * HF + tt * TTW:h * HF + (tt + 1) * TTW],
                                     start=(m == 0), stop=(m == 7))
                            ins = t.matmul(pQ[:, tt * TTW:(tt + 1) * TTW], onesb[:, 0:1],
                                           sq1[:, g % 2, tt * TTW:(tt + 1) * TTW],
                                           start=(m == 0), stop=(m == 7))
                        ins.then_inc(s_mm.sem, 1)
                    else:
                        t.wait_ge(s_sc.sem, leC[1])
                        for tt in range(NT):
                            t.matmul(psA[:, tt * TTW:(tt + 1) * TTW], onesr[:, :],
                                     meant[:, tt * TTW:(tt + 1) * TTW], start=True, stop=True)
                        ins = None
                        for tt in range(NT):
                            ins = t.matmul(psB[:, tt * TTW:(tt + 1) * TTW], onesr[:, :],
                                           rstdt[:, tt * TTW:(tt + 1) * TTW], start=True, stop=True)
                        ins.then_inc(s_mm.sem, 1)

            @blk.vector
            def _(v):
                v.memset(onesb[:], 1.0)
                ins = v.memset(onesr[:], 1.0)
                ins.then_inc(s_ve.sem, 1)
                for g in range(NG):
                    h, m = divmod(g, 8)
                    par = g % 2
                    v.wait_ge(s_mm.sem, mmC[(h, m)])
                    ins = v.tensor_add(x_fm[:, m, h * HF:(h + 1) * HF], pmm[par],
                                       x_fm[:, m, h * HF:(h + 1) * HF])
                    ins.then_inc(s_ve.sem, 1)
                    if m == 7:
                        v.wait_ge(s_mm.sem, stC[(h, 7)])
                        v.tensor_scalar(meant[:, h * HF:(h + 1) * HF], pS, 1.0 / D,
                                        None, ALU.mult)
                        v.tensor_mul(vart[:, h * HF:(h + 1) * HF],
                                     meant[:, h * HF:(h + 1) * HF],
                                     meant[:, h * HF:(h + 1) * HF])
                        ins = v.scalar_tensor_tensor(vart[:, h * HF:(h + 1) * HF], pQ,
                                                     1.0 / D, vart[:, h * HF:(h + 1) * HF],
                                                     ALU.mult, ALU.subtract)
                        ins.then_inc(s_ve.sem, 1)
                v.wait_ge(s_mm.sem, bcastC)
                v.tensor_copy(mbc[:], psA[:, :])
                ins = v.tensor_copy(rbc[:], psB[:, :])
                ins.then_inc(s_ve.sem, 1)
                for m in range(8):
                    v.tensor_sub(x_fm[:, m, :], x_fm[:, m, :], mbc[:])
                    ins = v.tensor_mul(x_fm[:, m, :], x_fm[:, m, :], rbc[:])
                    ins.then_inc(s_ve.sem, 1)

            @blk.scalar
            def _(sc):
                for g in range(NG):
                    h, m = divmod(g, 8)
                    sc.wait_ge(s_ve.sem, residC[(h, m)])
                    if g >= 2:
                        sc.wait_ge(s_mm.sem, stC[divmod(g - 2, 8)])
                    ins = sc.activation(sq1[:, g % 2, :],
                                        x_fm[:, m, h * HF:(h + 1) * HF], AF.Square)
                    ins.then_inc(s_sc.sem, 1)
                    if m == 7:
                        sc.wait_ge(s_ve.sem, mcC[h])
                        sc.activation(vart[:, h * HF:(h + 1) * HF],
                                      vart[:, h * HF:(h + 1) * HF], AF.Ln, bias=EPS)
                        ins = sc.activation(rstdt[:, h * HF:(h + 1) * HF],
                                            vart[:, h * HF:(h + 1) * HF],
                                            AF.Exp, scale=-0.5)
                        ins.then_inc(s_sc.sem, 1)

        s_mm.n = bcastC
        s_ve.n = normC[-1]
        s_sc.n = leC[1]
        return normC, dict(bcastC=bcastC, bcevC=bcevC, leC=leC,
                           mmClast=mmC[(1, 7)])


def _phase_d1(nc, blk, s_cast, s_dma, s_mm, s_sc, s_ve, m_cast, plnorm,
              wgb, wub, pl_fm, h_fm, psA, psB, wdb, wdh, seamB, seamC):
    QN, QM = 8, 3
    QH = 384
    stk = contextlib.ExitStack()
    with stk:
        wgq = [stk.enter_context(nc.sbuf_tensor(f"wgq{i}", [P, DC, QH], BF16)) for i in range(2)]
        wuq = [stk.enter_context(nc.sbuf_tensor(f"wuq{i}", [P, DC, QH], BF16)) for i in range(2)]
        gst = stk.enter_context(nc.sbuf_tensor("gst", [P, L], BF16))

        gdone = [s_mm.n + 2 * m + 1 for m in range(24)]
        udone = [s_mm.n + 2 * m + 2 for m in range(24)]
        hdone = [s_ve.n + m + 1 for m in range(24)]
        siluD = [s_sc.n + m + 1 for m in range(24)]
        base = s_dma.n
        wg_str = [[base + q * 256 + (k + 1) * 16 for k in range(DC)] for q in range(QN)]
        wu_str = [[base + q * 256 + 128 + (k + 1) * 16 for k in range(DC)] for q in range(QN)]
        wd0 = base + QN * 256
        wd_str = [[wd0 + hh * HC * 16 + (k + 1) * 16 for k in range(HC)]
                  for hh in range(2)]

        if True:
            @blk.sync
            def _(s):
                s_cast.wait(s, m_cast["wub"])
                s.wait_ge(s_ve.sem, seamB["veB"][23])
                s.wait_ge(s_sc.sem, seamB["ready"][23])
                s.wait_ge(s_mm.sem, seamC["mmClast"])
                for q in range(QN):
                    if q >= 2:
                        s.wait_ge(s_mm.sem, udone[QM * q - QM - 1])
                    for k in range(DC):
                        s_dma.dma2(s, lambda: s.dma_start_transpose(
                            wgq[q % 2][:, k, :], wgb[q * QH:(q + 1) * QH, k * P:(k + 1) * P]))
                    for k in range(DC):
                        s_dma.dma2(s, lambda: s.dma_start_transpose(
                            wuq[q % 2][:, k, :], wub[q * QH:(q + 1) * QH, k * P:(k + 1) * P]))
                s_cast.wait(s, m_cast["wdb"])
                for hh in range(2):
                    for k in range(HC):
                        s_dma.dma2(s, lambda: s.dma_start_transpose(
                            wdh[hh][:, k, :],
                            wdb[hh * (D // 2):(hh + 1) * (D // 2), k * P:(k + 1) * P]))

            @blk.tensor
            def _(t):
                for m in range(24):
                    q, mq = m // QM, m % QM
                    par = q % 2
                    if m >= 1:
                        t.wait_ge(s_sc.sem, siluD[m - 1])
                    ins = None
                    for k in range(DC):
                        if mq == 0:
                            s_dma.wait(t, wg_str[q][k])
                            if m == 0:
                                t.wait_ge(s_ve.sem, plnorm[k])
                        for tt in range(NT):
                            ins = t.matmul(
                                psA[:, tt * TTW:(tt + 1) * TTW],
                                wgq[par][:, k, mq * P:(mq + 1) * P],
                                pl_fm[:, k, tt * TTW:(tt + 1) * TTW],
                                start=(k == 0), stop=(k == DC - 1),
                            )
                    ins.then_inc(s_mm.sem, 1)
                    if m >= 1:
                        t.wait_ge(s_ve.sem, hdone[m - 1])
                    for k in range(DC):
                        if mq == 0:
                            s_dma.wait(t, wu_str[q][k])
                        for tt in range(NT):
                            ins = t.matmul(
                                psB[:, tt * TTW:(tt + 1) * TTW],
                                wuq[par][:, k, mq * P:(mq + 1) * P],
                                pl_fm[:, k, tt * TTW:(tt + 1) * TTW],
                                start=(k == 0), stop=(k == DC - 1),
                            )
                    ins.then_inc(s_mm.sem, 1)

            @blk.scalar
            def _(sc):
                for m in range(24):
                    sc.wait_ge(s_mm.sem, gdone[m])
                    if m >= 1:
                        sc.wait_ge(s_ve.sem, hdone[m - 1])
                    ins = sc.activation(gst[:], psA[:, :], AF.Silu,
                                        scale=SCALE)
                    ins.then_inc(s_sc.sem, 1)

            @blk.vector
            def _(v):
                for m in range(24):
                    v.wait_ge(s_sc.sem, siluD[m])
                    v.wait_ge(s_mm.sem, udone[m])
                    ins = v.scalar_tensor_tensor(h_fm[:, m, :], psB[:, :],
                                                 SCALE, gst[:],
                                                 ALU.mult, ALU.mult)
                    ins.then_inc(s_ve.sem, 1)

        s_mm.n = udone[-1]
        s_ve.n = hdone[-1]
        s_sc.n = siluD[-1]
        s_dma.n = wd_str[-1][-1]
        return dict(hdone=hdone, siluD=siluD, udone=udone, wd_str=wd_str)


def _phase_d2e(nc, blk, s_cast, s_dma, s_mm, s_sc, s_ve, m_cast,
               wdb, outb, h_fm, pl_fm, out, ps, wdh, seamD, ident):
    psA, psB = ps
    NTB = 4
    TB = L // NTB
    NSL = 12
    stk = contextlib.ExitStack()
    with stk:
        r2f = stk.enter_context(nc.sbuf_tensor("r2f", [P, NSL, TB], BF16))
        sq2 = stk.enter_context(nc.sbuf_tensor("sq2", [P, 2, TB], BF16))
        ofb = stk.enter_context(nc.sbuf_tensor("ofb", [P, 4, TB], BF16))
        otm = stk.enter_context(nc.sbuf_tensor("otm", [P, 2, D], BF16))
        onesb2 = stk.enter_context(nc.sbuf_tensor("onesb2", [P, 1], BF16))
        onesr2 = stk.enter_context(nc.sbuf_tensor("onesr2", [1, P], BF16))
        meant2 = stk.enter_context(nc.sbuf_tensor("meant2", [1, TB], BF16))
        rstd2 = stk.enter_context(nc.sbuf_tensor("rstd2", [1, TB], BF16))
        vart2 = stk.enter_context(nc.sbuf_tensor("vart2", [1, TB], F32))
        otf = nc.alloc_sbuf_tensor_at("otf", (P, 1, D), F32,
                                      offset=nc.lookup_mloc(otm).addr)
        ot3 = nc.alloc_sbuf_tensor_at("ot3", (P, 1, D), BF16,
                                      offset=nc.lookup_mloc(sq2).addr)

        pmm = [psA[:, 0:TB], psA[:, TB:2 * TB]]
        strip = [psA[:, 0:2 * TB], psB[:, 0:2 * TB], psB[:, 2 * TB:4 * TB]]
        pMB = psA[:, 2 * TB:3 * TB]
        pRB = psA[:, 3 * TB:4 * TB]
        pS = psB[0:1, 0:TB]
        pQ = psB[0:1, TB:2 * TB]
        slot = lambda tb, m: (tb * 8 + m) % NSL

        t_ops = []
        pend = []
        for gm in range(NTB * 8 + 6):
            if gm < NTB * 8:
                t_ops.append(("G",) + divmod(gm, 8))
            for it in list(pend):
                if it[3] <= gm:
                    t_ops.append(it[:3])
                    pend.remove(it)
            tb, m = divmod(gm, 8)
            if gm < NTB * 8:
                pend.append(("S", tb, m, gm + 2))
                if m == 7:
                    pend.append(("BC", tb, 0, gm + 4))
        mmF, stF, bcF = {}, {}, {}
        n = s_mm.n
        for op in t_ops:
            n += 1
            if op[0] == "G":
                mmF[(op[1], op[2])] = n
            elif op[0] == "S":
                stF[(op[1], op[2])] = n
            else:
                bcF[op[1]] = n

        v_ops = []
        for tb in range(NTB):
            if tb == 0:
                v_ops += [("EV", 0, m) for m in range(8)]
                v_ops.append(("MC", 0, 0))
            else:
                v_ops += [("EV", tb, 0), ("EV", tb, 1), ("EV", tb, 2),
                          ("NM", tb - 1, 0), ("NM", tb - 1, 1), ("EV", tb, 3),
                          ("NM", tb - 1, 2), ("NM", tb - 1, 3), ("EV", tb, 4),
                          ("NM", tb - 1, 4), ("NM", tb - 1, 5), ("EV", tb, 5),
                          ("EV", tb, 6),
                          ("NM", tb - 1, 6), ("NM", tb - 1, 7), ("EV", tb, 7)]
                v_ops.append(("MC", tb, 0))
        v_ops += [("NM", NTB - 1, m) for m in range(8)]
        evF, vvF, nmF = {}, {}, {}
        n = s_ve.n + 1
        for op in v_ops:
            n += 1
            d = {"EV": evF, "NM": nmF}.get(op[0])
            if d is not None:
                d[(op[1], op[2])] = n
            else:
                vvF[op[1]] = n

        c_ops = []
        for tb in range(NTB):
            if tb == 0:
                c_ops += [("SQ", 0, m) for m in range(8)]
            else:
                c_ops += [("SQ", tb, 0), ("SQ", tb, 1)]
                c_ops.append(("LE", tb - 1, 0))
                c_ops += [("SQ", tb, m) for m in range(2, 8)]
        c_ops.append(("LE", NTB - 1, 0))
        sqF, leF = {}, {}
        n = s_sc.n
        for op in c_ops:
            n += 1
            if op[0] == "SQ":
                sqF[(op[1], op[2])] = n
            else:
                leF[op[1]] = n
        ev3 = [n + st + 1 for st in range(3)]

        obF, trF, stS = {}, {}, {}
        n = s_dma.n
        for tb in range(NTB):
            for m in range(8):
                n += 16
                obF[(tb, m)] = n
            if tb < NTB - 1:
                for c in range(4):
                    n += 16
                    trF[(tb, c)] = n
        n += 16
        trF[(NTB - 1, 3)] = n
        for st in range(3):
            n += 16
            stS[st] = n
        ocF = {}
        n2 = s_cast.n
        for tb in range(NTB - 1):
            for c in range(4):
                n2 += 16
                ocF[(tb, c)] = n2
        n2 += 16
        ocF[(NTB - 1, 3)] = n2
        trb = {}

        if True:
            @blk.sync
            def _(s):
                for tb in range(NTB - 1):
                    for m in range(8):
                        s.wait_ge(s_ve.sem, nmF[(tb, m)])
                        s_dma.dma2(s, lambda: s.dma_start(
                            out=outb[m * P:(m + 1) * P, tb * TB:(tb + 1) * TB],
                            in_=ofb[:, m % 4, :]))
                    for mm_ in range(8):
                        s_dma.wait(s, obF[(tb, mm_)])
                    for c in range(4):
                        if c >= 2:
                            s_cast.wait(s, ocF[(tb, c - 2)])
                        elif tb >= 1:
                            s_cast.wait(s, ocF[(tb - 1, c + 2)])
                        s_dma.dma2(s, lambda: s.dma_start_transpose(
                            otm[:, c % 2, :], outb[:, (tb * 4 + c) * P:(tb * 4 + c + 1) * P]))
                tb = NTB - 1
                for m in range(8):
                    s.wait_ge(s_ve.sem, nmF[(tb, m)])
                    s_dma.dma2(s, lambda: s.dma_start(
                        out=outb[m * P:(m + 1) * P, tb * TB + 3 * P:(tb + 1) * TB],
                        in_=ofb[:, m % 4, 3 * P:TB]))
                for mm_ in range(8):
                    s_dma.wait(s, obF[(tb, mm_)])
                s.wait_ge(s_mm.sem, stF[(tb, 7)])
                s_dma.dma2(s, lambda: s.dma_start_transpose(
                    ot3[:, 0, :], outb[:, (tb * 4 + 3) * P:(tb * 4 + 4) * P]))
                for st in range(3):
                    s.wait_ge(s_sc.sem, ev3[st])
                    s_dma.dma2(s, lambda: s.dma_start(
                        out=out[(tb * 4 + st) * P:(tb * 4 + st + 1) * P, :],
                        in_=otf[:, 0, :]))
                    s_dma.wait(s, stS[st])

            @blk.gpsimd
            def _(g):
                for tb in range(NTB - 1):
                    for c in range(4):
                        s_dma.wait(g, trF[(tb, c)])
                        s_cast.dma2(g, lambda: g.dma_start(
                            out=out[(tb * 4 + c) * P:(tb * 4 + c + 1) * P, :],
                            in_=otm[:, c % 2, :]))
                s_dma.wait(g, trF[(NTB - 1, 3)])
                s_cast.dma2(g, lambda: g.dma_start(
                    out=out[(NTB * 4 - 1) * P:NTB * 4 * P, :], in_=ot3[:, 0, :]))
                s_cast.wait_all(g)

            @blk.tensor
            def _(t):
                for op in t_ops:
                    kind, tb, m = op
                    gm = tb * 8 + m
                    if kind == "G":
                        par = gm % 2
                        hh, mh = divmod(m, 4)
                        if gm == 0:
                            t.wait_ge(s_sc.sem, seamD["siluD"][23])
                        if gm >= 2:
                            t.wait_ge(s_ve.sem, evF[divmod(gm - 2, 8)])
                        ins = None
                        for k in range(HC):
                            if gm == 0:
                                t.wait_ge(s_ve.sem, seamD["hdone"][k])
                                s_dma.wait(t, seamD["wd_str"][0][k])
                            elif gm < 8 and mh == 0:
                                s_dma.wait(t, seamD["wd_str"][hh][k])
                            ins = t.matmul(
                                pmm[par],
                                wdh[hh][:, k, mh * P:(mh + 1) * P],
                                h_fm[:, k, tb * TB:(tb + 1) * TB],
                                start=(k == 0), stop=(k == HC - 1),
                            )
                        ins.then_inc(s_mm.sem, 1)
                    elif kind == "S":
                        t.wait_ge(s_ve.sem, evF[(tb, m)])
                        t.wait_ge(s_sc.sem, sqF[(tb, m)])
                        if m == 0 and tb >= 1:
                            t.wait_ge(s_ve.sem, vvF[tb - 1])
                        t.matmul(pS, onesb2[:, 0:1], r2f[:, slot(tb, m), :],
                                 start=(m == 0), stop=(m == 7))
                        ins = t.matmul(pQ, onesb2[:, 0:1], sq2[:, m % 2, :],
                                       start=(m == 0), stop=(m == 7))
                        ins.then_inc(s_mm.sem, 1)
                    else:
                        t.wait_ge(s_sc.sem, leF[tb])
                        if tb >= 1:
                            t.wait_ge(s_ve.sem, nmF[(tb - 1, 7)])
                        t.matmul(pMB, onesr2[:, :], meant2[:, :], start=True, stop=True)
                        ins = t.matmul(pRB, onesr2[:, :], rstd2[:, :], start=True, stop=True)
                        ins.then_inc(s_mm.sem, 1)
                tb3 = NTB - 1
                nmm = max(list(bcF.values()) + list(stF.values())
                          + list(mmF.values()))
                for m in range(8):
                    trb[m] = nmm + m + 1
                    t.wait_ge(s_ve.sem, nmF[(tb3, m)])
                    ins = None
                    for st in range(3):
                        ins = t.matmul(
                            strip[st][:, m * P:(m + 1) * P],
                            ofb[:, m % 4, st * P:(st + 1) * P], ident[:],
                            start=True, stop=True)
                    ins.then_inc(s_mm.sem, 1)

            @blk.vector
            def _(v):
                v.memset(onesb2[:], 1.0)
                ins = v.memset(onesr2[:], 1.0)
                ins.then_inc(s_ve.sem, 1)
                for op in v_ops:
                    kind, tb, x = op
                    if kind == "EV":
                        gm = tb * 8 + x
                        v.wait_ge(s_mm.sem, mmF[(tb, x)])
                        ins = v.scalar_tensor_tensor(
                            r2f[:, slot(tb, x), :], pmm[gm % 2], SCALE,
                            pl_fm[:, x, tb * TB:(tb + 1) * TB],
                            ALU.mult, ALU.add)
                        ins.then_inc(s_ve.sem, 1)
                    elif kind == "MC":
                        v.wait_ge(s_mm.sem, stF[(tb, 7)])
                        v.tensor_scalar(meant2[:], pS, 1.0 / D, None, ALU.mult)
                        v.tensor_mul(vart2[:], meant2[:], meant2[:])
                        ins = v.scalar_tensor_tensor(vart2[:], pQ, 1.0 / D,
                                                     vart2[:], ALU.mult, ALU.subtract)
                        ins.then_inc(s_ve.sem, 1)
                    elif kind == "NM":
                        m = x
                        if m == 0:
                            v.wait_ge(s_mm.sem, bcF[tb])
                        if (tb, m - 4) in obF:
                            s_dma.wait(v, obF[(tb, m - 4)])
                            if tb == NTB - 1:
                                v.wait_ge(s_mm.sem, trb[m - 4])
                        elif tb >= 1 and m < 4:
                            s_dma.wait(v, obF[(tb - 1, m + 4)])
                        v.tensor_sub(ofb[:, m % 4, :], r2f[:, slot(tb, m), :], pMB)
                        ins = v.tensor_mul(ofb[:, m % 4, :], ofb[:, m % 4, :], pRB)
                        ins.then_inc(s_ve.sem, 1)


            @blk.scalar
            def _(sc):
                for op in c_ops:
                    kind, tb, m = op
                    if kind == "SQ":
                        sc.wait_ge(s_ve.sem, evF[(tb, m)])
                        if (tb, m - 2) in stF and (tb * 8 + m) >= 2:
                            sc.wait_ge(s_mm.sem, stF[(tb, m - 2)])
                        elif tb >= 1 and m < 2:
                            sc.wait_ge(s_mm.sem, stF[(tb - 1, m + 6)])
                        ins = sc.activation(sq2[:, m % 2, :],
                                            r2f[:, slot(tb, m), :], AF.Square)
                        ins.then_inc(s_sc.sem, 1)
                    else:
                        sc.wait_ge(s_ve.sem, vvF[tb])
                        sc.activation(vart2[:], vart2[:], AF.Ln, bias=EPS)
                        ins = sc.activation(rstd2[:], vart2[:], AF.Exp, scale=-0.5)
                        ins.then_inc(s_sc.sem, 1)
                for st in range(3):
                    if st == 0:
                        sc.wait_ge(s_mm.sem, trb[7])
                        s_cast.wait(sc, ocF[(NTB - 2, 3)])
                    else:
                        s_dma.wait(sc, stS[st - 1])
                    ins = sc.activation(otf[:, 0, :], strip[st][:, 0:D],
                                        AF.Copy)
                    ins.then_inc(s_sc.sem, 1)

        s_mm.n = trb[7]
        s_ve.n = nmF[(NTB - 1, 7)]
        s_sc.n = leF[NTB - 1] + 3
        s_dma.n = stS[2]
        s_cast.n = ocF[(NTB - 1, 3)]


_NC_CACHE = None


def kernel(L0_attn_x, L0_attn_W_in, L0_attn_W_out, L0_attn_conv_w, L0_attn_conv_b,
           L0_attn_dt_w, L0_attn_dt_b, L0_attn_A, L0_attn_Dp,
           L0_ffn_w_gate, L0_ffn_w_up, L0_ffn_w_down):
    global _NC_CACHE
    x = np.ascontiguousarray(np.asarray(L0_attn_x, dtype=np.float32))
    com = dict(
        w_in=np.ascontiguousarray(np.asarray(L0_attn_W_in, dtype=np.float32)),
        w_out=np.ascontiguousarray(np.asarray(L0_attn_W_out, dtype=np.float32)),
        conv_w=np.ascontiguousarray(np.asarray(L0_attn_conv_w, dtype=np.float32)),
        conv_b=np.ascontiguousarray(np.asarray(L0_attn_conv_b, dtype=np.float32)),
        dt_w=np.ascontiguousarray(np.asarray(L0_attn_dt_w, dtype=np.float32)),
        dt_b=np.ascontiguousarray(np.asarray(L0_attn_dt_b, dtype=np.float32)),
        A=np.ascontiguousarray(np.asarray(L0_attn_A, dtype=np.float32)),
        Dp=np.ascontiguousarray(np.asarray(L0_attn_Dp, dtype=np.float32)),
        wg=np.ascontiguousarray(np.asarray(L0_ffn_w_gate, dtype=np.int32)),
        wu=np.ascontiguousarray(np.asarray(L0_ffn_w_up, dtype=np.int32)),
        wd=np.ascontiguousarray(np.asarray(L0_ffn_w_down, dtype=np.int32)),
    )
    if _NC_CACHE is None:
        _NC_CACHE = build()
    in_maps = [dict(x=np.ascontiguousarray(x[b]), **com) for b in range(8)]
    res = run_bass_kernel_spmd(_NC_CACHE, in_maps, core_ids=list(range(8)))
    return np.stack([np.asarray(r["out"]) for r in res.results], axis=0)
